# version 20
# baseline (speedup 1.0000x reference)
import contextlib

import numpy as np

import concourse.bass as bass
import concourse.mybir as mybir
from concourse.bass_utils import run_bass_kernel_spmd
from concourse.hw_specs import get_activation_tables

F32 = mybir.dt.float32
F32R = mybir.dt.float32r
BF16 = mybir.dt.bfloat16
I8 = mybir.dt.int8
AL = mybir.AluOpType
ACT = mybir.ActivationFunctionType
AX = mybir.AxisListType.X

D = 1024
NC = 8
H = 16
HD = 64
TT = 512
LN_EPS = 1e-5
DENOM_EPS = 1e-6
Q_EPS = 1e-5
MAGIC = 12582912.0


class Ch:

    def __init__(self, sem):
        self.sem = sem
        self.n = 0

    def inc(self, instr, k=1):
        instr.then_inc(self.sem, k)
        self.n += k
        return self.n


class DQ:

    def __init__(self, bld, sems):
        self.bld = bld
        self.ch = [Ch(sems[0]), Ch(sems[1])]

    def issue(self, eng, ename, i, instr):
        ch = self.ch[i % 2]
        self.bld.W(eng, ename, ch, ch.n)
        instr.then_inc(ch.sem, 16)
        ch.n += 16

    def wait(self, eng, ename, i):
        if i < 0:
            return
        self.bld.W(eng, ename, self.ch[i % 2], 16 * (i // 2 + 1))

    def total(self):
        return [(c.sem, c.n) for c in self.ch]


class Builder:
    def __init__(self, L, n_cores):
        self.L = L
        self.NT = L // TT
        self.n_cores = n_cores
        self.waited = {}

    def W(self, eng, ename, ch, val=None):
        v = ch.n if val is None else val
        if v <= 0:
            return
        key = (ename, ch.sem.num)
        if self.waited.get(key, -1) < v:
            eng.wait_ge(ch.sem, v)
            self.waited[key] = v

    def build(self):
        L, NT = self.L, self.NT
        nc = bass.Bass("TRN2", target_bir_lowering=False, debug=False)
        self.nc = nc

        xT = nc.dram_tensor("xT", [D, L], F32, kind="ExternalInput")
        wqkv = nc.dram_tensor("wqkv", [D, 3 * D], F32, kind="ExternalInput")
        wout = nc.dram_tensor("wout", [D, D], F32, kind="ExternalInput")
        wi8T = nc.dram_tensor("wi8T", [D, D], I8, kind="ExternalInput")
        gam = nc.dram_tensor("gam", [D], F32, kind="ExternalInput")
        bet = nc.dram_tensor("bet", [D], F32, kind="ExternalInput")
        fbias = nc.dram_tensor("fbias", [D], F32, kind="ExternalInput")
        wsc = nc.dram_tensor("wsc", [D], F32, kind="ExternalInput")
        outT = nc.dram_tensor("outT", [D, L], F32, kind="ExternalOutput")
        x2d = nc.dram_tensor("x2d", [NC, 128, L], F32)
        redd = nc.dram_tensor("redd", [128], F32)
        selc = nc.dram_tensor("selc", [16, NC * 128], F32, kind="ExternalInput")
        cc_in = nc.dram_tensor("cc_in", [1, 8], F32)
        cc_out = nc.dram_tensor("cc_out", [1, 8], F32)

        xT_t = xT.ap().rearrange("(c p) (i t) -> i p c t", p=128, t=TT)
        outT_t = outT.ap().rearrange("(c p) (i t) -> i p c t", p=128, t=TT)
        x2d_t = x2d.ap().rearrange("c p (i t) -> i p c t", t=TT)

        act_set_id = None
        for idx, (name, funcs) in enumerate(get_activation_tables(nc.m.arch).items()):
            fs = {str(f).split(".")[-1] for f in funcs}
            if {"Copy", "Square", "Ln", "Exp"} <= fs:
                act_set_id = idx
                break

        es = contextlib.ExitStack()
        with es:
            gsb = es.enter_context(nc.sbuf_tensor("gsb", [128, NC], F32))
            bsb = es.enter_context(nc.sbuf_tensor("bsb", [128, NC], F32))
            fbsb = es.enter_context(nc.sbuf_tensor("fbsb", [128, NC], F32))
            wssb = es.enter_context(nc.sbuf_tensor("wssb", [128, NC], F32))
            swsc = es.enter_context(nc.sbuf_tensor("swsc", [128, NC], F32))
            ones = es.enter_context(nc.sbuf_tensor("ones", [128, 128], F32))
            onesr = es.enter_context(nc.sbuf_tensor("onesr", [128, 128], F32R))
            selE = es.enter_context(nc.sbuf_tensor("selE", [16, NC * 128], F32R))
            mxac = es.enter_context(nc.sbuf_tensor("mxac", [128, NT * NC + 8], F32))
            mxrow = es.enter_context(nc.sbuf_tensor("mxrow", [1, 128], F32))
            mx1 = es.enter_context(nc.sbuf_tensor("mx1", [1, 8], F32))
            xscq = es.enter_context(nc.sbuf_tensor("xscq", [128, 2], F32))
            cvec = es.enter_context(nc.sbuf_tensor("cvec", [128, 6], F32))
            esQST = contextlib.ExitStack()
            qst = esQST.enter_context(nc.sbuf_tensor("qst", [128, NC, L], BF16))
            kvks = esQST.enter_context(nc.sbuf_tensor("kvks", [128, NC, 65], BF16))
            bdg = esQST.enter_context(nc.sbuf_tensor("bdg", [128, NC, 16], BF16))
            s_xa0 = es.enter_context(nc.semaphore("s_xa0"))
            s_xa1 = es.enter_context(nc.semaphore("s_xa1"))
            s_xb0 = es.enter_context(nc.semaphore("s_xb0"))
            s_xb1 = es.enter_context(nc.semaphore("s_xb1"))
            s_w0 = es.enter_context(nc.semaphore("s_w0"))
            s_w1 = es.enter_context(nc.semaphore("s_w1"))
            s_x2r0 = es.enter_context(nc.semaphore("s_x2r0"))
            s_x2r1 = es.enter_context(nc.semaphore("s_x2r1"))
            s_o0 = es.enter_context(nc.semaphore("s_o0"))
            s_o1 = es.enter_context(nc.semaphore("s_o1"))
            s_wv = es.enter_context(nc.semaphore("s_wv"))
            s_wq = es.enter_context(nc.semaphore("s_wq"))
            s_wo = es.enter_context(nc.semaphore("s_wo"))
            s_wi = es.enter_context(nc.semaphore("s_wi"))
            s_gm = es.enter_context(nc.semaphore("s_gm"))
            c_t = es.enter_context(nc.semaphore("c_t"))
            c_v = es.enter_context(nc.semaphore("c_v"))
            c_s = es.enter_context(nc.semaphore("c_s"))
            c_cc = es.enter_context(nc.semaphore("c_cc"))
            pbar = es.enter_context(nc.semaphore("pbar"))
            block = es.enter_context(nc.Block())
            self.block = block
            tch, vch, sch = Ch(c_t), Ch(c_v), Ch(c_s)
            sgm = Ch(s_gm)
            swv, swq, swo, swi = Ch(s_wv), Ch(s_wq), Ch(s_wo), Ch(s_wi)
            dq_xa = DQ(self, (s_xa0, s_xa1))
            dq_xb = DQ(self, (s_xb0, s_xb1))
            dq_x2w = DQ(self, (s_w0, s_w1))
            dq_x2r = DQ(self, (s_x2r0, s_x2r1))
            dq_out = DQ(self, (s_o0, s_o1))
            self.tch, self.vch, self.sch = tch, vch, sch
            pb = Ch(pbar)

            T = block.tensor
            V = block.vector
            S = block.scalar
            G = block.gpsimd
            Y = block.sync

            def vop(v, emit):
                self.W(v, "v", vch)
                vch.inc(emit())

            def sop(s, emit):
                self.W(s, "s", sch)
                sch.inc(emit())

            def g_setup(g):
                g.dma_start(out=selE[:, :], in_=selc.ap()).then_inc(s_wv, 16)
                g.dma_start(out=gsb[:, :], in_=gam.ap().rearrange("(c p) -> p c", p=128),
                            allow_slow_non_contiguous=True).then_inc(s_wv, 16)
                g.dma_start(out=bsb[:, :], in_=bet.ap().rearrange("(c p) -> p c", p=128),
                            allow_slow_non_contiguous=True).then_inc(s_wv, 16)
                g.dma_start(out=fbsb[:, :], in_=fbias.ap().rearrange("(c p) -> p c", p=128),
                            allow_slow_non_contiguous=True).then_inc(s_wv, 16)
                g.dma_start(out=wssb[:, :], in_=wsc.ap().rearrange("(c p) -> p c", p=128),
                            allow_slow_non_contiguous=True).then_inc(s_wv, 16)
                swv.n += 80

            G(g_setup)

            def v_setup(v):
                vop(v, lambda: v.memset(ones[:, :], 1.0))
                vop(v, lambda: v.tensor_copy(onesr[:, :], ones[:, :]))
                vop(v, lambda: v.memset(mx1[:, :], 0.0))
                vop(v, lambda: v.memset(cvec[:, 0:1], LN_EPS))
                vop(v, lambda: v.memset(cvec[:, 1:2], 1.0 / D))
                vop(v, lambda: v.memset(cvec[:, 2:3], -0.5))
                vop(v, lambda: v.memset(cvec[:, 3:4], -1.0))
                vop(v, lambda: v.memset(cvec[:, 4:5], DENOM_EPS))
                vop(v, lambda: v.memset(cvec[:, 5:6], -MAGIC))
                vop(v, lambda: v.memset(mxac[:, :], 0.0))

            V(v_setup)
            setup_tok = vch.n

            if act_set_id is not None:
                def s_pre(s, idx=act_set_id):
                    s.add_instruction(mybir.InstLoadActFuncSet(
                        name=nc.get_next_instruction_name(),
                        act_func_set_id=idx, ins=[], outs=[]))

                S(s_pre)

            esA = contextlib.ExitStack()
            with esA:
                wqbf = esA.enter_context(nc.sbuf_tensor("wqbf", [128, NC, 3 * D], BF16))
                xtlA = esA.enter_context(nc.sbuf_tensor("xtlA", [128, 2, NC, TT], F32))
                xr = esA.enter_context(nc.sbuf_tensor("xr", [128, NC, TT], F32R))
                xnbf = esA.enter_context(nc.sbuf_tensor("xnbf", [128, 2, NC, TT], BF16))
                kbf = esA.enter_context(nc.sbuf_tensor("kbf", [128, 2, 2 * TT], BF16))
                v65 = esA.enter_context(nc.sbuf_tensor("v65", [128, 2, 16, 65], BF16))
                mbA = esA.enter_context(nc.sbuf_tensor("mbA", [128, TT], F32))
                t1A = esA.enter_context(nc.sbuf_tensor("t1A", [128, TT], F32))
                t2A = esA.enter_context(nc.sbuf_tensor("t2A", [128, TT], F32))
                riA = esA.enter_context(nc.sbuf_tensor("riA", [128, TT], F32))
                exA = esA.enter_context(nc.sbuf_tensor("exA", [128, 2, TT], F32))
                psA0 = esA.enter_context(nc.psum_tensor("psA0", [128, TT], F32))
                psA1 = esA.enter_context(nc.psum_tensor("psA1", [128, TT], F32))
                psA2 = esA.enter_context(nc.psum_tensor("psA2", [128, TT], F32))
                sums = esA.enter_context(nc.psum_tensor("sums", [128, TT], F32))
                sumsq = esA.enter_context(nc.psum_tensor("sumsq", [128, TT], F32))
                kvacc0 = esA.enter_context(nc.psum_tensor("kvacc0", [128, 288], F32))
                kvacc1 = esA.enter_context(nc.psum_tensor("kvacc1", [128, 288], F32))
                kvsum = esA.enter_context(nc.sbuf_tensor("kvsum", [128, NC, 65], F32))
                psA = [psA0, psA1, psA2]
                psA_free = [0, 0, 0]
                stats_free = [0]
                kbf_free = [0, 0]
                kvacc_free = [0]
                sumsq_tok = [0]
                xa_free = [0] * (NT + 2)
                xn_tok = [0] * (NT + 1)
                exfree = [0, 0]

                def v65_ones(v):
                    for b in range(2):
                        vop(v, lambda b=b: v.memset(v65[:, b, :, 64:65], 1.0))
                    vop(v, lambda: v.memset(kvsum[:, :, :], 0.0))

                V(v65_ones)

                def g_wq(g):
                    for kc in range(NC):
                        g.dma_start(out=wqbf[:, kc, :],
                                    in_=wqkv.ap()[kc * 128:(kc + 1) * 128, :]).then_inc(s_wq, 16)
                        swq.n += 16

                G(g_wq)

                def a_load(i):
                    if i >= NT:
                        return

                    def y_load(y, i=i):
                        if i >= 2:
                            self.W(y, "y", vch, xa_free[i - 2])
                        dq_xa.issue(y, "y", i,
                                    y.dma_start(out=xtlA[:, i % 2, :, :], in_=xT_t[i]))

                    Y(y_load)

                def a_stats(i):
                    def s_xr(s, i=i):
                        dq_xa.wait(s, "s", i)
                        if i >= 1:
                            self.W(s, "s", tch, sumsq_tok[0])
                        for kc in range(NC):
                            sop(s, lambda kc=kc: s.activation(
                                xr[:, kc, :], xtlA[:, i % 2, kc, :], ACT.Copy))

                    S(s_xr)
                    xr_tok = sch.n

                    def t_sums(t):
                        self.W(t, "t", sch, xr_tok)
                        self.W(t, "t", vch, max(setup_tok, stats_free[0]))
                        for kc in range(NC):
                            mm = t.matmul(sums[:, :], onesr[:, :], xr[:, kc, :],
                                          start=(kc == 0), stop=(kc == NC - 1))
                        tch.inc(mm)

                    T(t_sums)

                    def s_sq(s):
                        self.W(s, "s", tch)
                        for kc in range(NC):
                            sop(s, lambda kc=kc: s.activation(
                                xr[:, kc, :], xr[:, kc, :], ACT.Square))

                    S(s_sq)
                    sq_tok = sch.n

                    def t_sumsq(t):
                        self.W(t, "t", sch, sq_tok)
                        for kc in range(NC):
                            mm = t.matmul(sumsq[:, :], onesr[:, :], xr[:, kc, :],
                                          start=(kc == 0), stop=(kc == NC - 1))
                        tch.inc(mm)

                    T(t_sumsq)
                    sumsq_tok[0] = tch.n

                def a_smalls(i):
                    stats_tok = tch.n

                    def v_ln1(v):
                        self.W(v, "v", tch, stats_tok)
                        vop(v, lambda: v.tensor_scalar(mbA[:, :], sums[:, :], 1.0 / D, None, AL.mult))
                        vop(v, lambda: v.tensor_tensor(t1A[:, :], mbA[:, :], sums[:, :], AL.mult))
                        vop(v, lambda: v.tensor_tensor(t2A[:, :], sumsq[:, :], t1A[:, :], AL.subtract))

                    V(v_ln1)
                    ln1_tok = vch.n
                    stats_free[0] = vch.n

                    def s_sd(s):
                        self.W(s, "s", vch, ln1_tok)
                        sop(s, lambda: s.activation(t2A[:, :], t2A[:, :], ACT.Ln,
                                                    bias=cvec[:, 0:1], scale=cvec[:, 1:2]))
                        sop(s, lambda: s.activation(riA[:, :], t2A[:, :], ACT.Exp,
                                                    bias=0.0, scale=cvec[:, 2:3]))

                    S(s_sd)
                    return sch.n

                def a_apply(i, sd_tok, kc0, kc1):
                    def v_apply(v, i=i):
                        self.W(v, "v", swv, 80)
                        self.W(v, "v", sch, sd_tok)
                        for kc in range(kc0, kc1):
                            vop(v, lambda kc=kc: v.scalar_tensor_tensor(
                                t1A[:, :], xtlA[:, i % 2, kc, :], 1.0, mbA[:, :],
                                AL.mult, AL.subtract))
                            vop(v, lambda kc=kc: v.tensor_tensor(
                                t2A[:, :], t1A[:, :], riA[:, :], AL.mult))
                            vop(v, lambda kc=kc: v.tensor_scalar(
                                xnbf[:, i % 2, kc, :], t2A[:, :], gsb[:, kc:kc + 1],
                                bsb[:, kc:kc + 1], AL.mult, AL.add))

                    V(v_apply)
                    if kc1 == NC:
                        xn_tok[i] = vch.n
                        xa_free[i] = vch.n

                def a_qunits(i):
                    for ncx in range(NC):
                        slot = ncx % 3

                        def t_q(t, i=i, ncx=ncx, slot=slot):
                            self.W(t, "t", vch, xn_tok[i])
                            self.W(t, "t", swq, 16 * NC)
                            self.W(t, "t", vch, psA_free[slot])
                            for kc in range(NC):
                                mm = t.matmul(psA[slot][:, :],
                                              wqbf[:, kc, ncx * 128:(ncx + 1) * 128],
                                              xnbf[:, i % 2, kc, :],
                                              start=(kc == 0), stop=(kc == NC - 1))
                            tch.inc(mm)

                        T(t_q)
                        q_tok = tch.n
                        eb = ncx % 2

                        def s_exp(s, slot=slot, eb=eb):
                            self.W(s, "s", tch, q_tok)
                            if exfree[eb]:
                                self.W(s, "s", vch, exfree[eb])
                            sop(s, lambda: s.activation(exA[:, eb, :], psA[slot][:, :],
                                                        ACT.Exp))

                        S(s_exp)
                        ex_tok = sch.n

                        def v_q(v, i=i, ncx=ncx, slot=slot, eb=eb):
                            self.W(v, "v", sch, ex_tok)
                            vop(v, lambda: v.tensor_scalar(
                                exA[:, eb, :], exA[:, eb, :], 1.0, None, AL.min))
                            vop(v, lambda: v.scalar_tensor_tensor(
                                qst[:, ncx, i * TT:(i + 1) * TT], psA[slot][:, :], 1.0,
                                exA[:, eb, :], AL.add, AL.max))

                        V(v_q)
                        psA_free[slot] = vch.n
                        exfree[eb] = vch.n

                def a_kvgroup(i, g_):
                    ngr = TT // 128
                    gbuf = (i * ngr + g_) % 2
                    for piece in range(4):
                        slot = piece % 3

                        def t_kv(t, i=i, g_=g_, piece=piece, slot=slot, gbuf=gbuf):
                            self.W(t, "t", vch, xn_tok[i])
                            self.W(t, "t", vch, psA_free[slot])
                            if piece == 0:
                                self.W(t, "t", tch, kbf_free[gbuf])
                            for kc in range(NC):
                                mm = t.matmul(psA[slot][:, :],
                                              xnbf[:, i % 2, kc, g_ * 128:(g_ + 1) * 128],
                                              wqkv_rhs(kc, piece),
                                              start=(kc == 0), stop=(kc == NC - 1))
                            tch.inc(mm)

                        def wqkv_rhs(kc, piece):
                            return wqbf[:, kc, D + piece * 512:D + (piece + 1) * 512]

                        T(t_kv)
                        kv_tok = tch.n

                        if piece < 2:
                            eb = piece % 2

                            def s_exp2(s, slot=slot, eb=eb):
                                self.W(s, "s", tch, kv_tok)
                                if exfree[eb]:
                                    self.W(s, "s", vch, exfree[eb])
                                sop(s, lambda: s.activation(exA[:, eb, :],
                                                            psA[slot][:, :], ACT.Exp))

                            S(s_exp2)
                            ex2 = sch.n

                            def v_k(v, piece=piece, slot=slot, gbuf=gbuf, eb=eb):
                                self.W(v, "v", sch, ex2)
                                vop(v, lambda: v.tensor_scalar(
                                    exA[:, eb, :], exA[:, eb, :], 1.0, None, AL.min))
                                vop(v, lambda: v.scalar_tensor_tensor(
                                    kbf[:, gbuf, piece * 512:(piece + 1) * 512],
                                    psA[slot][:, :], 1.0, exA[:, eb, :], AL.add, AL.max))

                            V(v_k)
                            psA_free[slot] = vch.n
                            exfree[eb] = vch.n
                        else:
                            def v_v(v, piece=piece, slot=slot, gbuf=gbuf):
                                self.W(v, "v", tch, kv_tok)
                                p0 = (piece - 2) * 8
                                vop(v, lambda: v.tensor_copy(
                                    v65[:, gbuf, p0:p0 + 8, 0:64],
                                    psA[slot][:, :].rearrange("p (h j) -> p h j", j=64)))

                            V(v_v)
                            psA_free[slot] = vch.n

                    evac_tok = vch.n

                    def kvmm_emit(gbuf=gbuf, evac_tok=evac_tok):
                        def t_kvmm(t, gbuf=gbuf):
                            self.W(t, "t", vch, evac_tok)
                            self.W(t, "t", vch, kvacc_free[0])
                            for c in range(NC):
                                tgt = kvacc0 if c < 4 else kvacc1
                                cc = c % 4
                                for half in range(2):
                                    h = 2 * c + half
                                    mm = t.matmul(tgt[half * 64:(half + 1) * 64,
                                                      cc * 72:cc * 72 + 65],
                                                  kbf[:, gbuf, h * 64:(h + 1) * 64],
                                                  v65[:, gbuf, h, :],
                                                  start=True, stop=True)
                            tch.inc(mm)

                        T(t_kvmm)
                        kbf_free[gbuf] = tch.n
                        kvmm_tok = tch.n

                        def v_kvacc(v, kvmm_tok=kvmm_tok):
                            self.W(v, "v", tch, kvmm_tok)
                            for half2 in range(2):
                                tgt = kvacc0 if half2 == 0 else kvacc1
                                vop(v, lambda tgt=tgt, half2=half2: v.tensor_add(
                                    kvsum[:, half2 * 4:(half2 + 1) * 4, :],
                                    kvsum[:, half2 * 4:(half2 + 1) * 4, :],
                                    tgt[:, :].rearrange("p (c e) -> p c e", e=72)[:, :, 0:65]))

                        V(v_kvacc)
                        kvacc_free[0] = vch.n
                    return kvmm_emit

                a_load(0)
                a_load(1)
                a_stats(0)
                sd0 = a_smalls(0)
                a_apply(0, sd0, 0, NC)
                pend = [None]
                for i in range(NT):
                    a_qunits(i)
                    if i + 1 < NT:
                        a_stats(i + 1)
                        sdn = a_smalls(i + 1)
                    for g_ in range(4):
                        ke = a_kvgroup(i, g_)
                        if pend[0] is not None:
                            pend[0]()
                        pend[0] = ke
                        if g_ == 0 and i + 1 < NT:
                            a_apply(i + 1, sdn, 0, NC // 2)
                        if g_ == 1 and i + 1 < NT:
                            a_apply(i + 1, sdn, NC // 2, NC)
                            a_load(i + 2)
                if pend[0] is not None:
                    pend[0]()

                def v_kvev(v):
                    vop(v, lambda: v.tensor_copy(kvks[:, :, :], kvsum[:, :, :]))
                    vop(v, lambda: v.memset(bdg[:, :, :], 0.0))
                    for h in range(16):
                        half = h % 2
                        c = h // 2
                        vop(v, lambda h=h, half=half, c=c: v.tensor_copy(
                            bdg[half * 64:(half + 1) * 64, c, h:h + 1],
                            kvks[half * 64:(half + 1) * 64, c, 64:65]))

                V(v_kvev)

            self._barrier(pb, 5)

            esB = contextlib.ExitStack()
            with esB:
                wor = esB.enter_context(nc.sbuf_tensor("wor", [128, NC, D], F32R))
                xtlB = esB.enter_context(nc.sbuf_tensor("xtlB", [128, 2, NC, TT], F32))
                atr = esB.enter_context(nc.sbuf_tensor("atr", [128, NC, TT], F32R))
                ysb = esB.enter_context(nc.sbuf_tensor("ysb", [128, NC, TT], F32))
                x2tl = esB.enter_context(nc.sbuf_tensor("x2tl", [128, 1, NC, TT], F32))
                yr = atr
                rcd = esB.enter_context(nc.sbuf_tensor("rcd", [16, TT], F32))
                rcdr = esB.enter_context(nc.sbuf_tensor("rcdr", [16, 2, TT], F32R))
                mbB = esB.enter_context(nc.sbuf_tensor("mbB", [128, TT], F32))
                t1B = esB.enter_context(nc.sbuf_tensor("t1B", [128, 2, TT], F32))
                t2B = esB.enter_context(nc.sbuf_tensor("t2B", [128, TT], F32))
                riB = esB.enter_context(nc.sbuf_tensor("riB", [128, TT], F32))
                psB0 = esB.enter_context(nc.psum_tensor("psB0", [128, TT], F32))
                psB1 = esB.enter_context(nc.psum_tensor("psB1", [128, TT], F32))
                psB2 = esB.enter_context(nc.psum_tensor("psB2", [128, TT], F32))
                psB3 = esB.enter_context(nc.psum_tensor("psB3", [128, TT], F32))
                psB4 = esB.enter_context(nc.psum_tensor("psB4", [128, TT], F32))
                denp = esB.enter_context(nc.psum_tensor("denp", [16, TT], F32))
                sumsB = esB.enter_context(nc.psum_tensor("sumsB", [128, TT], F32))
                sumsqB = esB.enter_context(nc.psum_tensor("sumsqB", [128, TT], F32))
                psB = [psB0, psB1, psB2, psB3, psB4]
                psB_free = [(vch, 0)] * 5
                statsB_free = [(vch, 0)]
                denp_free = [(sch, 0)]
                t1f = [(vch, 0), (vch, 0)]
                xb_free = [0] * (NT + 2)
                rcd_tok = [0] * (NT + 2)
                stats_tok = [0] * (NT + 1)
                y_toks = [[0] * NC for _ in range(NT)]

                def g_wo(g):
                    for ec in range(NC):
                        g.dma_start(out=wor[:, ec, :],
                                    in_=wout.ap()[ec * 128:(ec + 1) * 128, :]).then_inc(s_wo, 16)
                        swo.n += 16

                G(g_wo)

                def b_load(i):
                    if i >= NT:
                        return

                    def y_loadB(y, i=i):
                        if i >= 2:
                            self.W(y, "y", vch, xb_free[i - 2])
                        dq_xb.issue(y, "y", i,
                                    y.dma_start(out=xtlB[:, i % 2, :, :], in_=xT_t[i]))

                    Y(y_loadB)

                def b_den(i):
                    if i >= NT:
                        return

                    def t_den(t, i=i):
                        ch0, v0 = denp_free[0]
                        if v0:
                            self.W(t, "t", ch0, v0)
                        for kc in range(NC):
                            mm = t.matmul(denp[:, :], bdg[:, kc, :],
                                          qst[:, kc, i * TT:(i + 1) * TT],
                                          start=(kc == 0), stop=(kc == NC - 1))
                        tch.inc(mm)

                    T(t_den)
                    den_tok = tch.n

                    def s_recip(s, i=i, den_tok=den_tok):
                        self.W(s, "s", tch, den_tok)
                        sop(s, lambda: s.activation(rcd[:, :], denp[:, :], ACT.Ln,
                                                    bias=cvec[0:16, 4:5], scale=1.0))
                        sop(s, lambda: s.activation(rcdr[:, i % 2, :], rcd[:, :],
                                                    ACT.Exp, bias=0.0,
                                                    scale=cvec[0:16, 3:4]))

                    S(s_recip)
                    rcd_tok[i] = sch.n
                    denp_free[0] = (sch, sch.n)

                def b_pairs(i):
                    for c in range(NC):
                        s0 = c % 2
                        s1 = 2 + (c % 2)

                        def t_pair(t, i=i, c=c, s0=s0, s1=s1):
                            ch0, v0 = psB_free[s0]
                            if v0:
                                self.W(t, "t", ch0, v0)
                            for half in range(2):
                                mm = t.matmul(psB[s0][half * 64:(half + 1) * 64, :],
                                              kvks[half * 64:(half + 1) * 64, c, 0:64],
                                              qst[half * 64:(half + 1) * 64, c,
                                                  i * TT:(i + 1) * TT],
                                              start=True, stop=True)
                                tch.inc(mm)
                            self.W(t, "t", swv, 80)
                            self.W(t, "t", sch, rcd_tok[i])
                            ch1, v1 = psB_free[s1]
                            if v1:
                                self.W(t, "t", ch1, v1)
                            mm = t.matmul(psB[s1][:, :],
                                          selE[:, c * 128:(c + 1) * 128],
                                          rcdr[:, i % 2, :], start=True, stop=True)
                            tch.inc(mm)

                        T(t_pair)
                        pair_tok = tch.n

                        def s_dc(s, c=c, s1=s1, pair_tok=pair_tok):
                            self.W(s, "s", tch, pair_tok)
                            ch2, v2 = t1f[c % 2]
                            if v2:
                                self.W(s, "s", ch2, v2)
                            sop(s, lambda: s.activation(t1B[:, c % 2, :],
                                                        psB[s1][:, :], ACT.Copy))

                        S(s_dc)
                        dc_tok = sch.n
                        psB_free[s1] = (sch, dc_tok)

                        def v_div(v, c=c, s0=s0, dc_tok=dc_tok):
                            self.W(v, "v", sch, dc_tok)
                            vop(v, lambda: v.tensor_tensor(
                                atr[:, c, :], psB[s0][:, :], t1B[:, c % 2, :],
                                AL.mult))

                        V(v_div)
                        psB_free[s0] = (vch, vch.n)
                        t1f[c % 2] = (vch, vch.n)

                def b_wout(i, at_tok):
                    wout_last = [0]
                    for fc in range(NC):
                        slot = fc % 5

                        def t_wout(t, fc=fc, slot=slot):
                            self.W(t, "t", vch, at_tok)
                            self.W(t, "t", swo, 16 * NC)
                            ch0, v0 = psB_free[slot]
                            if v0:
                                self.W(t, "t", ch0, v0)
                            for ec in range(NC):
                                mm = t.matmul(psB[slot][:, :],
                                              wor[:, ec, fc * 128:(fc + 1) * 128],
                                              atr[:, ec, :],
                                              start=(ec == 0), stop=(ec == NC - 1))
                            tch.inc(mm)

                        T(t_wout)
                        wout_tok = tch.n
                        wout_last[0] = wout_tok

                        def v_y(v, i=i, fc=fc, slot=slot):
                            self.W(v, "v", tch, wout_tok)
                            dq_xb.wait(v, "v", i)
                            vop(v, lambda: v.scalar_tensor_tensor(
                                ysb[:, fc, :], xtlB[:, i % 2, fc, :], 2.0,
                                psB[slot][:, :], AL.mult, AL.add))

                        V(v_y)
                        psB_free[slot] = (vch, vch.n)
                        y_toks[i][fc] = vch.n
                    xb_free[i] = vch.n
                    return wout_last[0]

                def b_stats(i, wout_all):
                    yr_toks = []
                    sums_toks = []
                    ysq_toks = []

                    def s_yr(s, i=i):
                        self.W(s, "s", tch, wout_all)
                        for kc in range(NC):
                            self.W(s, "s", vch, y_toks[i][kc])
                            sop(s, lambda kc=kc: s.activation(
                                yr[:, kc, :], ysb[:, kc, :], ACT.Copy))
                            yr_toks.append(sch.n)

                    S(s_yr)

                    def t_sumsB(t):
                        ch0, v0 = statsB_free[0]
                        if v0:
                            self.W(t, "t", ch0, v0)
                        for kc in range(NC):
                            self.W(t, "t", sch, yr_toks[kc])
                            mm = t.matmul(sumsB[:, :], onesr[:, :], yr[:, kc, :],
                                          start=(kc == 0), stop=(kc == NC - 1))
                            tch.inc(mm)
                            sums_toks.append(tch.n)

                    T(t_sumsB)

                    def s_ysq(s):
                        for kc in range(NC):
                            self.W(s, "s", tch, sums_toks[kc])
                            sop(s, lambda kc=kc: s.activation(
                                yr[:, kc, :], yr[:, kc, :], ACT.Square))
                            ysq_toks.append(sch.n)

                    S(s_ysq)

                    def t_sumsqB(t):
                        for kc in range(NC):
                            self.W(t, "t", sch, ysq_toks[kc])
                            mm = t.matmul(sumsqB[:, :], onesr[:, :], yr[:, kc, :],
                                          start=(kc == 0), stop=(kc == NC - 1))
                            tch.inc(mm)

                    T(t_sumsqB)
                    stats_tok[i] = tch.n

                def b_tail(i):
                    if i < 0:
                        return

                    def v_ln2(v, i=i):
                        self.W(v, "v", tch, stats_tok[i])
                        vop(v, lambda: v.tensor_scalar(mbB[:, :], sumsB[:, :], 1.0 / D, None, AL.mult))
                        vop(v, lambda: v.tensor_tensor(riB[:, :], mbB[:, :], sumsB[:, :], AL.mult))
                        vop(v, lambda: v.tensor_tensor(t2B[:, :], sumsqB[:, :], riB[:, :], AL.subtract))

                    V(v_ln2)
                    ln2_tok = vch.n
                    statsB_free[0] = (vch, vch.n)

                    def s_sdB(s):
                        self.W(s, "s", vch, ln2_tok)
                        sop(s, lambda: s.activation(t2B[:, :], t2B[:, :], ACT.Ln,
                                                    bias=cvec[:, 0:1], scale=cvec[:, 1:2]))
                        sop(s, lambda: s.activation(riB[:, :], t2B[:, :], ACT.Exp,
                                                    bias=0.0, scale=cvec[:, 2:3]))

                    S(s_sdB)
                    sdB_tok = sch.n

                    def v_x2(v, i=i):
                        self.W(v, "v", sch, sdB_tok)
                        dq_x2w.wait(v, "v", i - 1)
                        for kc in range(NC):
                            vop(v, lambda kc=kc: v.scalar_tensor_tensor(
                                t2B[:, :], ysb[:, kc, :], 1.0, mbB[:, :],
                                AL.mult, AL.subtract))
                            vop(v, lambda kc=kc: v.tensor_tensor(
                                x2tl[:, 0, kc, :], t2B[:, :], riB[:, :], AL.mult))

                    V(v_x2)
                    x2_tok = vch.n

                    def y_x2w(y, i=i):
                        self.W(y, "y", vch, x2_tok)
                        dq_x2w.issue(y, "y", i, y.dma_start(out=x2d_t[i], in_=x2tl[:, 0, :, :]))

                    Y(y_x2w)

                    def v_x2r(v, i=i):
                        vop(v, lambda: v.tensor_reduce(
                            mxac[:, i:i + 1],
                            x2tl[:, 0, :, :].rearrange("p c t -> p (c t)"),
                            AX, AL.max, apply_absolute_value=True))

                    V(v_x2r)

                b_load(0)
                b_load(1)
                b_den(0)
                b_den(1)
                for i in range(NT):
                    b_pairs(i)
                    div_tok = vch.n
                    b_tail(i - 1)
                    wout_all = b_wout(i, div_tok)
                    b_stats(i, wout_all)
                    b_den(i + 2)
                    b_load(i + 2)
                b_tail(NT - 1)

                def v_mx(v):
                    vop(v, lambda: v.tensor_reduce(mxac[:, NT * NC:NT * NC + 1],
                                                   mxac[:, 0:NT], AX, AL.max))

                V(v_mx)
                mx_tok = vch.n

                def g_flip(g):
                    self.W(g, "g", vch, mx_tok)
                    g.dma_start(out=redd.ap().rearrange("(p o) -> p o", o=1),
                                in_=mxac[:, NT * NC:NT * NC + 1]).then_inc(s_gm, 16)
                    sgm.n += 16
                    g.wait_ge(s_gm, sgm.n)
                    g.dma_start(out=mxrow[:, :],
                                in_=redd.ap().rearrange("(o p) -> o p", o=1)).then_inc(s_gm, 16)
                    sgm.n += 16

                G(g_flip)
                flip_tok = sgm.n

                def v_gmax(v):
                    self.W(v, "v", sgm, flip_tok)
                    vop(v, lambda: v.tensor_reduce(mx1[0:1, 0:1], mxrow[:, :], AX, AL.max))

                V(v_gmax)
                gmax_tok = vch.n

                def g_cc(g):
                    self.W(g, "g", vch, gmax_tok)
                    g.dma_start(out=cc_in[:, :], in_=mx1[:, :]).then_inc(s_gm, 16)
                    sgm.n += 16
                    g.wait_ge(s_gm, sgm.n)
                    g.collective_compute(
                        "AllReduce", AL.max,
                        replica_groups=[list(range(self.n_cores))],
                        ins=[cc_in.ap().opt()], outs=[cc_out.ap().opt()],
                    ).then_inc(c_cc, 1)
                    g.wait_ge(c_cc, 1)
                    g.dma_start(out=mx1[:, :], in_=cc_out[:, :]).then_inc(s_gm, 16)
                    sgm.n += 16

                G(g_cc)
                cc_tok = sgm.n

                def v_xsc(v):
                    self.W(v, "v", sgm, cc_tok)
                    vop(v, lambda: v.tensor_scalar(mx1[0:1, 1:2], mx1[0:1, 0:1],
                                                   1.0 / 127.0, Q_EPS, AL.mult, AL.max))
                    vop(v, lambda: v.reciprocal(mx1[0:1, 2:3], mx1[0:1, 1:2]))

                V(v_xsc)
                xsc_tok = vch.n

                def t_xscb(t):
                    self.W(t, "t", vch, xsc_tok)
                    mm = t.matmul(sumsB[:, 0:2], ones[0:1, :],
                                  mx1[0:1, 1:3], start=True, stop=True)
                    tch.inc(mm)

                T(t_xscb)
                xscb_tok = tch.n

                def v_xsc2(v):
                    self.W(v, "v", tch, xscb_tok)
                    vop(v, lambda: v.tensor_copy(xscq[:, :], sumsB[:, 0:2]))
                    self.W(v, "v", swv, 80)
                    vop(v, lambda: v.tensor_scalar(swsc[:, :], wssb[:, :],
                                                   xscq[:, 0:1], None, AL.mult))

                V(v_xsc2)

            esQST.close()

            self._barrier(pb, 10, dqs=(dq_x2w,))

            esC = contextlib.ExitStack()
            with esC:
                wibf = esC.enter_context(nc.sbuf_tensor("wibf", [128, NC, D], BF16))
                x2tC = esC.enter_context(nc.sbuf_tensor("x2tC", [128, 2, NC, TT], F32))
                qsc = esC.enter_context(nc.sbuf_tensor("qsc", [128, NC, TT], F32))
                wibf_ = wibf

                def g_wi(g):
                    for kc in range(NC):
                        g.dma_start(out=wibf_[:, kc, :],
                                    in_=wi8T.ap()[kc * 128:(kc + 1) * 128, :]).then_inc(s_wi, 16)
                        swi.n += 16

                G(g_wi)
                xqbf = esC.enter_context(nc.sbuf_tensor("xqbf", [128, 2, NC, TT], BF16))
                y3 = esC.enter_context(nc.sbuf_tensor("y3", [128, 2, NC, TT], F32))
                y3r = esC.enter_context(nc.sbuf_tensor("y3r", [128, NC, TT], F32R))
                y3sq = esC.enter_context(nc.sbuf_tensor("y3sq", [128, NC, TT], F32R))
                otl = esC.enter_context(nc.sbuf_tensor("otl", [128, 2, NC, TT], F32))
                mbC = esC.enter_context(nc.sbuf_tensor("mbC", [128, TT], F32))
                t1C = esC.enter_context(nc.sbuf_tensor("t1C", [128, TT], F32))
                t2C = esC.enter_context(nc.sbuf_tensor("t2C", [128, TT], F32))
                riC = esC.enter_context(nc.sbuf_tensor("riC", [128, TT], F32))
                psC = [esC.enter_context(nc.psum_tensor("psC%d" % k, [128, TT], F32))
                       for k in range(6)]
                sumsC = esC.enter_context(nc.psum_tensor("sumsC", [128, TT], F32))
                sumsqC = esC.enter_context(nc.psum_tensor("sumsqC", [128, TT], F32))
                psC_free = [0] * 6
                statsC_free = [(vch, 0)]
                x2c_free = [0] * (NT + 2)
                q_tok = [0] * (NT + 1)
                statsC_tok = [0] * (NT + 1)
                epi_toks = [[0] * NC for _ in range(NT)]
                y3r_toks = [[0] * NC for _ in range(NT)]
                sumsT_toks = [[0] * NC for _ in range(NT)]
                sumsqT_toks = [[0] * NC for _ in range(NT)]
                sdC_toks = [0] * NT
                sq_toks = [[0] * NC for _ in range(NT)]
                vq_toks = [[0] * NC for _ in range(NT)]
                qs_toks = [[0] * NC for _ in range(NT)]

                def c_load(i):
                    if i >= NT:
                        return

                    def y_loadC(y, i=i):
                        dq_x2w.wait(y, "y", i)
                        if i >= 2:
                            self.W(y, "y", vch, x2c_free[i - 2])
                        dq_x2r.issue(y, "y", i,
                                     y.dma_start(out=x2tC[:, i % 2, :, :], in_=x2d_t[i]))

                    Y(y_loadC)

                def c_quant_v(i):
                    if i >= NT:
                        return

                    def v_q(v, i=i):
                        dq_x2r.wait(v, "v", i)
                        for kc in range(NC):
                            if i >= 1 and qs_toks[i - 1][kc]:
                                self.W(v, "v", sch, qs_toks[i - 1][kc])
                            vop(v, lambda kc=kc: v.tensor_scalar(
                                qsc[:, kc, :], x2tC[:, i % 2, kc, :],
                                xscq[:, 1:2], MAGIC, AL.mult, AL.add))
                            vq_toks[i][kc] = vch.n

                    V(v_q)

                def c_quant_s(i):
                    if i >= NT:
                        return

                    def s_q(s, i=i):
                        for kc in range(NC):
                            self.W(s, "s", vch, vq_toks[i][kc])
                            sop(s, lambda kc=kc: s.activation(
                                xqbf[:, i % 2, kc, :], qsc[:, kc, :],
                                ACT.Identity, bias=cvec[:, 5:6], scale=1.0))
                            qs_toks[i][kc] = sch.n

                    S(s_q)
                    q_tok[i] = sch.n

                def c_ffn(i):
                    for ncx in range(NC):
                        slot = ncx % 6

                        def t_ffn(t, i=i, ncx=ncx, slot=slot):
                            self.W(t, "t", sch, q_tok[i])
                            self.W(t, "t", swi, 16 * NC)
                            self.W(t, "t", vch, psC_free[slot])
                            for kc in range(NC):
                                mm = t.matmul(psC[slot][:, :],
                                              wibf[:, kc, ncx * 128:(ncx + 1) * 128],
                                              xqbf[:, i % 2, kc, :],
                                              start=(kc == 0), stop=(kc == NC - 1))
                            tch.inc(mm)

                        T(t_ffn)
                        ffn_tok = tch.n

                        def v_epi(v, i=i, ncx=ncx, slot=slot):
                            self.W(v, "v", tch, ffn_tok)
                            if i >= 2 and y3r_toks[i - 2][ncx]:
                                self.W(v, "v", sch, y3r_toks[i - 2][ncx])
                            vop(v, lambda: v.scalar_tensor_tensor(
                                y3[:, i % 2, ncx, :], psC[slot][:, :],
                                swsc[:, ncx:ncx + 1],
                                x2tC[:, i % 2, ncx, :], AL.mult, AL.add))

                        V(v_epi)
                        psC_free[slot] = vch.n
                        epi_toks[i][ncx] = vch.n
                    x2c_free[i] = vch.n

                def c_stats_s(i):
                    def s_y3r(s, i=i):
                        for kc in range(NC):
                            self.W(s, "s", vch, epi_toks[i][kc])
                            if i >= 1 and sumsT_toks[i - 1][kc]:
                                self.W(s, "s", tch, sumsT_toks[i - 1][kc])
                            sop(s, lambda kc=kc: s.activation(
                                y3r[:, kc, :], y3[:, i % 2, kc, :],
                                ACT.Identity, bias=fbsb[:, kc:kc + 1], scale=1.0))
                            y3r_toks[i][kc] = sch.n
                            if i >= 1 and sumsqT_toks[i - 1][kc]:
                                self.W(s, "s", tch, sumsqT_toks[i - 1][kc])
                            sop(s, lambda kc=kc: s.activation(
                                y3sq[:, kc, :], y3r[:, kc, :], ACT.Square))
                            sq_toks[i][kc] = sch.n

                    S(s_y3r)

                def c_stats_t(i):
                    if i < 0:
                        return

                    def t_sumsC(t, i=i):
                        ch0, v0 = statsC_free[0]
                        if v0:
                            self.W(t, "t", ch0, v0)
                        for kc in range(NC):
                            self.W(t, "t", sch, y3r_toks[i][kc])
                            mm = t.matmul(sumsC[:, :], onesr[:, :], y3r[:, kc, :],
                                          start=(kc == 0), stop=(kc == NC - 1))
                            tch.inc(mm)
                            sumsT_toks[i][kc] = tch.n
                        for kc in range(NC):
                            self.W(t, "t", sch, sq_toks[i][kc])
                            mm = t.matmul(sumsqC[:, :], onesr[:, :], y3sq[:, kc, :],
                                          start=(kc == 0), stop=(kc == NC - 1))
                            tch.inc(mm)
                            sumsqT_toks[i][kc] = tch.n

                    T(t_sumsC)
                    statsC_tok[i] = tch.n

                def c_tail_ln(i):
                    if i < 0:
                        return

                    def v_ln3(v, i=i):
                        self.W(v, "v", tch, statsC_tok[i])
                        vop(v, lambda: v.tensor_scalar(mbC[:, :], sumsC[:, :], 1.0 / D, None, AL.mult))
                        vop(v, lambda: v.tensor_tensor(t1C[:, :], mbC[:, :], sumsC[:, :], AL.mult))
                        vop(v, lambda: v.tensor_tensor(t2C[:, :], sumsqC[:, :], t1C[:, :], AL.subtract))

                    V(v_ln3)
                    ln3_tok = vch.n
                    statsC_free[0] = (vch, vch.n)

                    def s_sdC(s):
                        self.W(s, "s", vch, ln3_tok)
                        sop(s, lambda: s.activation(t2C[:, :], t2C[:, :], ACT.Ln,
                                                    bias=cvec[:, 0:1], scale=cvec[:, 1:2]))
                        sop(s, lambda: s.activation(riC[:, :], t2C[:, :], ACT.Exp,
                                                    bias=0.0, scale=cvec[:, 2:3]))

                    S(s_sdC)
                    sdC_toks[i] = sch.n

                def c_tail_out(i):
                    if i < 0:
                        return
                    sdC_tok = sdC_toks[i]

                    def v_out(v, i=i):
                        self.W(v, "v", sch, sdC_tok)
                        dq_out.wait(v, "v", i - 2)
                        for kc in range(NC):
                            vop(v, lambda kc=kc, i=i: v.scalar_tensor_tensor(
                                t1C[:, :], y3[:, i % 2, kc, :], fbsb[:, kc:kc + 1],
                                mbC[:, :], AL.add, AL.subtract))
                            vop(v, lambda kc=kc, i=i: v.tensor_tensor(
                                otl[:, i % 2, kc, :], t1C[:, :], riC[:, :], AL.mult))

                    V(v_out)
                    out_tok = vch.n

                    def y_store(y, i=i):
                        self.W(y, "y", vch, out_tok)
                        dq_out.issue(y, "y", i, y.dma_start(out=outT_t[i], in_=otl[:, i % 2, :, :]))

                    Y(y_store)

                c_load(0)
                c_load(1)
                c_quant_v(0)
                c_quant_s(0)
                for i in range(NT):
                    c_ffn(i)
                    c_quant_v(i + 1)
                    c_quant_s(i + 1)
                    c_stats_t(i - 1)
                    c_tail_ln(i - 1)
                    c_stats_s(i)
                    c_tail_out(i - 1)
                    c_load(i + 2)
                c_stats_t(NT - 1)
                c_tail_ln(NT - 1)
                c_tail_out(NT - 1)

            def g_fin(g):
                for dq in (dq_out, dq_xa, dq_xb, dq_x2w, dq_x2r):
                    for sem, n in dq.total():
                        if n:
                            g.wait_ge(sem, n)
                for ch in (swv, swq, swo, swi, sgm):
                    if ch.n:
                        g.wait_ge(ch.sem, ch.n)

            G(g_fin)

        return nc

    def _barrier(self, pb, target, dqs=()):
        chains = {"t": self.tch, "v": self.vch, "s": self.sch}
        for name, sec in (("t", self.block.tensor), ("v", self.block.vector),
                          ("s", self.block.scalar), ("g", self.block.gpsimd),
                          ("y", self.block.sync)):
            def f(eng, pb=pb, name=name):
                ch = chains.get(name)
                if ch is not None:
                    self.W(eng, name, ch)
                if name == "y":
                    for dq in dqs:
                        for sem, n in dq.total():
                            if n:
                                eng.wait_ge(sem, n)
                eng.nop().then_inc(pb.sem, 1)
            sec(f)
        pb.n += 5

        for name, sec in (("t", self.block.tensor), ("v", self.block.vector),
                          ("s", self.block.scalar), ("g", self.block.gpsimd),
                          ("y", self.block.sync)):
            def f(eng, pb=pb, target=target):
                eng.wait_ge(pb.sem, target)
            sec(f)


def build_nc(L=4096, n_cores=8):
    return Builder(L, n_cores).build()


_NC_CACHE = {}


def kernel(L0_attn_x, L0_attn_gamma, L0_attn_beta, L0_attn_Wqkv, L0_attn_Wout,
           L0_ffn_weight_int8, L0_ffn_weight_scale, L0_ffn_bias):
    x = np.asarray(L0_attn_x)
    B, L, Dm = x.shape
    assert Dm == D
    n_cores = 8
    key = (L, n_cores)
    if key not in _NC_CACHE:
        _NC_CACHE[key] = build_nc(L=L, n_cores=n_cores)
    nc = _NC_CACHE[key]

    wqkv = np.ascontiguousarray(np.asarray(L0_attn_Wqkv, dtype=np.float32))
    wout = np.ascontiguousarray(np.asarray(L0_attn_Wout, dtype=np.float32))
    wi8T = np.ascontiguousarray(np.asarray(L0_ffn_weight_int8).T)
    gam = np.ascontiguousarray(np.asarray(L0_attn_gamma, dtype=np.float32))
    bet = np.ascontiguousarray(np.asarray(L0_attn_beta, dtype=np.float32))
    fbias = np.ascontiguousarray(np.asarray(L0_ffn_bias, dtype=np.float32))
    wsc = np.ascontiguousarray(np.asarray(L0_ffn_weight_scale, dtype=np.float32).reshape(-1))

    selc = np.zeros((16, D), dtype=np.float32)
    for c in range(NC):
        selc[2 * c, c * 128:c * 128 + 64] = 1.0
        selc[2 * c + 1, c * 128 + 64:(c + 1) * 128] = 1.0

    in_maps = []
    for c in range(n_cores):
        in_maps.append({
            "xT": np.ascontiguousarray(x[c].T.astype(np.float32, copy=False)),
            "wqkv": wqkv, "wout": wout, "wi8T": wi8T,
            "gam": gam, "bet": bet, "fbias": fbias, "wsc": wsc, "selc": selc,
        })
    res = run_bass_kernel_spmd(nc, in_maps, core_ids=list(range(n_cores)))
    out = np.stack([np.ascontiguousarray(r["outT"].T) for r in res.results], axis=0)
    return out.astype(np.float32, copy=False)


# version 22
# speedup vs baseline: 1.0265x; 1.0265x over previous
import contextlib

import numpy as np

import concourse.bass as bass
import concourse.mybir as mybir
from concourse.bass_utils import run_bass_kernel_spmd
from concourse.hw_specs import get_activation_tables

F32 = mybir.dt.float32
F32R = mybir.dt.float32r
BF16 = mybir.dt.bfloat16
I8 = mybir.dt.int8
AL = mybir.AluOpType
ACT = mybir.ActivationFunctionType
AX = mybir.AxisListType.X

D = 1024
NC = 8
H = 16
HD = 64
TT = 512
LN_EPS = 1e-5
DENOM_EPS = 1e-6
Q_EPS = 1e-5
MAGIC = 12582912.0


class Ch:

    def __init__(self, sem):
        self.sem = sem
        self.n = 0

    def inc(self, instr, k=1):
        instr.then_inc(self.sem, k)
        self.n += k
        return self.n


class DQ:

    def __init__(self, bld, sems):
        self.bld = bld
        self.ch = [Ch(sems[0]), Ch(sems[1])]

    def issue(self, eng, ename, i, instr):
        ch = self.ch[i % 2]
        self.bld.W(eng, ename, ch, ch.n)
        instr.then_inc(ch.sem, 16)
        ch.n += 16

    def wait(self, eng, ename, i):
        if i < 0:
            return
        self.bld.W(eng, ename, self.ch[i % 2], 16 * (i // 2 + 1))

    def total(self):
        return [(c.sem, c.n) for c in self.ch]


class Builder:
    def __init__(self, L, n_cores):
        self.L = L
        self.NT = L // TT
        self.n_cores = n_cores
        self.waited = {}

    def W(self, eng, ename, ch, val=None):
        v = ch.n if val is None else val
        if v <= 0:
            return
        key = (ename, ch.sem.num)
        if self.waited.get(key, -1) < v:
            eng.wait_ge(ch.sem, v)
            self.waited[key] = v

    def build(self):
        L, NT = self.L, self.NT
        nc = bass.Bass("TRN2", target_bir_lowering=False, debug=False)
        self.nc = nc

        xT = nc.dram_tensor("xT", [D, L], F32, kind="ExternalInput")
        wqkv = nc.dram_tensor("wqkv", [D, 3 * D], F32, kind="ExternalInput")
        wout = nc.dram_tensor("wout", [D, D], F32, kind="ExternalInput")
        wi8T = nc.dram_tensor("wi8T", [D, D], I8, kind="ExternalInput")
        gam = nc.dram_tensor("gam", [D], F32, kind="ExternalInput")
        bet = nc.dram_tensor("bet", [D], F32, kind="ExternalInput")
        fbias = nc.dram_tensor("fbias", [D], F32, kind="ExternalInput")
        wsc = nc.dram_tensor("wsc", [D], F32, kind="ExternalInput")
        outT = nc.dram_tensor("outT", [D, L], F32, kind="ExternalOutput")
        x2d = nc.dram_tensor("x2d", [NC, 128, L], F32)
        redd = nc.dram_tensor("redd", [128], F32)
        selc = nc.dram_tensor("selc", [16, NC * 128], F32, kind="ExternalInput")
        cc_in = nc.dram_tensor("cc_in", [1, 8], F32)
        cc_out = nc.dram_tensor("cc_out", [1, 8], F32)

        xT_t = xT.ap().rearrange("(c p) (i t) -> i p c t", p=128, t=TT)
        outT_t = outT.ap().rearrange("(c p) (i t) -> i p c t", p=128, t=TT)
        x2d_t = x2d.ap().rearrange("c p (i t) -> i p c t", t=TT)

        act_set_id = None
        for idx, (name, funcs) in enumerate(get_activation_tables(nc.m.arch).items()):
            fs = {str(f).split(".")[-1] for f in funcs}
            if {"Copy", "Square", "Ln", "Exp"} <= fs:
                act_set_id = idx
                break

        es = contextlib.ExitStack()
        with es:
            gsb = es.enter_context(nc.sbuf_tensor("gsb", [128, NC], F32))
            bsb = es.enter_context(nc.sbuf_tensor("bsb", [128, NC], F32))
            fbsb = es.enter_context(nc.sbuf_tensor("fbsb", [128, NC], F32))
            wssb = es.enter_context(nc.sbuf_tensor("wssb", [128, NC], F32))
            swsc = es.enter_context(nc.sbuf_tensor("swsc", [128, NC], F32))
            ones = es.enter_context(nc.sbuf_tensor("ones", [128, 128], F32))
            onesr = es.enter_context(nc.sbuf_tensor("onesr", [128, 128], F32R))
            selE = es.enter_context(nc.sbuf_tensor("selE", [16, NC * 128], F32R))
            mxac = es.enter_context(nc.sbuf_tensor("mxac", [128, NT * NC + 8], F32))
            mxrow = es.enter_context(nc.sbuf_tensor("mxrow", [1, 128], F32))
            mx1 = es.enter_context(nc.sbuf_tensor("mx1", [1, 8], F32))
            xscq = es.enter_context(nc.sbuf_tensor("xscq", [128, 2], F32))
            cvec = es.enter_context(nc.sbuf_tensor("cvec", [128, 6], F32))
            esQST = contextlib.ExitStack()
            qst = esQST.enter_context(nc.sbuf_tensor("qst", [128, NC, L], BF16))
            kvks = esQST.enter_context(nc.sbuf_tensor("kvks", [128, NC, 65], BF16))
            bdg = esQST.enter_context(nc.sbuf_tensor("bdg", [128, NC, 16], BF16))
            s_xa0 = es.enter_context(nc.semaphore("s_xa0"))
            s_xa1 = es.enter_context(nc.semaphore("s_xa1"))
            s_xb0 = es.enter_context(nc.semaphore("s_xb0"))
            s_xb1 = es.enter_context(nc.semaphore("s_xb1"))
            s_w0 = es.enter_context(nc.semaphore("s_w0"))
            s_w1 = es.enter_context(nc.semaphore("s_w1"))
            s_x2r0 = es.enter_context(nc.semaphore("s_x2r0"))
            s_x2r1 = es.enter_context(nc.semaphore("s_x2r1"))
            s_o0 = es.enter_context(nc.semaphore("s_o0"))
            s_o1 = es.enter_context(nc.semaphore("s_o1"))
            s_wv = es.enter_context(nc.semaphore("s_wv"))
            s_wq = es.enter_context(nc.semaphore("s_wq"))
            s_wo = es.enter_context(nc.semaphore("s_wo"))
            s_wi = es.enter_context(nc.semaphore("s_wi"))
            s_gm = es.enter_context(nc.semaphore("s_gm"))
            c_t = es.enter_context(nc.semaphore("c_t"))
            c_v = es.enter_context(nc.semaphore("c_v"))
            c_s = es.enter_context(nc.semaphore("c_s"))
            c_cc = es.enter_context(nc.semaphore("c_cc"))
            pbar = es.enter_context(nc.semaphore("pbar"))
            block = es.enter_context(nc.Block())
            self.block = block
            tch, vch, sch = Ch(c_t), Ch(c_v), Ch(c_s)
            sgm = Ch(s_gm)
            swv, swq, swo, swi = Ch(s_wv), Ch(s_wq), Ch(s_wo), Ch(s_wi)
            dq_xa = DQ(self, (s_xa0, s_xa1))
            dq_xb = DQ(self, (s_xb0, s_xb1))
            dq_x2w = DQ(self, (s_w0, s_w1))
            dq_x2r = DQ(self, (s_x2r0, s_x2r1))
            dq_out = DQ(self, (s_o0, s_o1))
            self.tch, self.vch, self.sch = tch, vch, sch
            pb = Ch(pbar)

            T = block.tensor
            V = block.vector
            S = block.scalar
            G = block.gpsimd
            Y = block.sync

            def vop(v, emit):
                self.W(v, "v", vch)
                vch.inc(emit())

            def sop(s, emit):
                self.W(s, "s", sch)
                sch.inc(emit())

            def g_setup(g):
                g.dma_start(out=selE[:, :], in_=selc.ap()).then_inc(s_wv, 16)
                g.dma_start(out=gsb[:, :], in_=gam.ap().rearrange("(c p) -> p c", p=128),
                            allow_slow_non_contiguous=True).then_inc(s_wv, 16)
                g.dma_start(out=bsb[:, :], in_=bet.ap().rearrange("(c p) -> p c", p=128),
                            allow_slow_non_contiguous=True).then_inc(s_wv, 16)
                g.dma_start(out=fbsb[:, :], in_=fbias.ap().rearrange("(c p) -> p c", p=128),
                            allow_slow_non_contiguous=True).then_inc(s_wv, 16)
                g.dma_start(out=wssb[:, :], in_=wsc.ap().rearrange("(c p) -> p c", p=128),
                            allow_slow_non_contiguous=True).then_inc(s_wv, 16)
                swv.n += 80

            G(g_setup)

            def v_setup(v):
                vop(v, lambda: v.memset(ones[:, :], 1.0))
                vop(v, lambda: v.tensor_copy(onesr[:, :], ones[:, :]))
                vop(v, lambda: v.memset(mx1[:, :], 0.0))
                vop(v, lambda: v.memset(cvec[:, 0:1], LN_EPS))
                vop(v, lambda: v.memset(cvec[:, 1:2], 1.0 / D))
                vop(v, lambda: v.memset(cvec[:, 2:3], -0.5))
                vop(v, lambda: v.memset(cvec[:, 3:4], -1.0))
                vop(v, lambda: v.memset(cvec[:, 4:5], DENOM_EPS))
                vop(v, lambda: v.memset(cvec[:, 5:6], -MAGIC))
                vop(v, lambda: v.memset(mxac[:, :], 0.0))

            V(v_setup)
            setup_tok = vch.n

            if act_set_id is not None:
                def s_pre(s, idx=act_set_id):
                    s.add_instruction(mybir.InstLoadActFuncSet(
                        name=nc.get_next_instruction_name(),
                        act_func_set_id=idx, ins=[], outs=[]))

                S(s_pre)

            esA = contextlib.ExitStack()
            with esA:
                wqbf = esA.enter_context(nc.sbuf_tensor("wqbf", [128, NC, 3 * D], BF16))
                xtlA = esA.enter_context(nc.sbuf_tensor("xtlA", [128, 2, NC, TT], F32))
                xr = esA.enter_context(nc.sbuf_tensor("xr", [128, NC, TT], F32R))
                xnbf = esA.enter_context(nc.sbuf_tensor("xnbf", [128, 2, NC, TT], BF16))
                kbf = esA.enter_context(nc.sbuf_tensor("kbf", [128, 2, 2 * TT], BF16))
                v65 = esA.enter_context(nc.sbuf_tensor("v65", [128, 2, 16, 65], BF16))
                mbA = esA.enter_context(nc.sbuf_tensor("mbA", [128, TT], F32))
                t1A = esA.enter_context(nc.sbuf_tensor("t1A", [128, TT], F32))
                t2A = esA.enter_context(nc.sbuf_tensor("t2A", [128, TT], F32))
                riA = esA.enter_context(nc.sbuf_tensor("riA", [128, TT], F32))
                exA = esA.enter_context(nc.sbuf_tensor("exA", [128, 3, TT], F32))
                psA0 = esA.enter_context(nc.psum_tensor("psA0", [128, TT], F32))
                psA1 = esA.enter_context(nc.psum_tensor("psA1", [128, TT], F32))
                psA2 = esA.enter_context(nc.psum_tensor("psA2", [128, TT], F32))
                psA3 = esA.enter_context(nc.psum_tensor("psA3", [128, TT], F32))
                sums = esA.enter_context(nc.psum_tensor("sums", [128, TT], F32))
                sumsq = esA.enter_context(nc.psum_tensor("sumsq", [128, TT], F32))
                kvacc0 = esA.enter_context(nc.psum_tensor("kvacc0", [128, 288], F32))
                kvacc1 = esA.enter_context(nc.psum_tensor("kvacc1", [128, 288], F32))
                kvsum = esA.enter_context(nc.sbuf_tensor("kvsum", [128, NC, 65], F32))
                psA = [psA0, psA1, psA2, psA3]
                psA_free = [0, 0, 0, 0]
                stats_free = [0]
                kbf_free = [0, 0]
                kvacc_free = [0]
                sumsq_tok = [0]
                xa_free = [0] * (NT + 2)
                xn_tok = [0] * (NT + 1)
                exfree = [0, 0, 0]
                exctr = [0]

                def v65_ones(v):
                    for b in range(2):
                        vop(v, lambda b=b: v.memset(v65[:, b, :, 64:65], 1.0))
                    vop(v, lambda: v.memset(kvsum[:, :, :], 0.0))

                V(v65_ones)

                def g_wq(g):
                    for kc in range(NC):
                        g.dma_start(out=wqbf[:, kc, :],
                                    in_=wqkv.ap()[kc * 128:(kc + 1) * 128, :]).then_inc(s_wq, 16)
                        swq.n += 16

                G(g_wq)

                def a_load(i):
                    if i >= NT:
                        return

                    def y_load(y, i=i):
                        if i >= 2:
                            self.W(y, "y", vch, xa_free[i - 2])
                        dq_xa.issue(y, "y", i,
                                    y.dma_start(out=xtlA[:, i % 2, :, :], in_=xT_t[i]))

                    Y(y_load)

                def a_stats(i):
                    def s_xr(s, i=i):
                        dq_xa.wait(s, "s", i)
                        if i >= 1:
                            self.W(s, "s", tch, sumsq_tok[0])
                        for kc in range(NC):
                            sop(s, lambda kc=kc: s.activation(
                                xr[:, kc, :], xtlA[:, i % 2, kc, :], ACT.Copy))

                    S(s_xr)
                    xr_tok = sch.n

                    def t_sums(t):
                        self.W(t, "t", sch, xr_tok)
                        self.W(t, "t", vch, max(setup_tok, stats_free[0]))
                        for kc in range(NC):
                            mm = t.matmul(sums[:, :], onesr[:, :], xr[:, kc, :],
                                          start=(kc == 0), stop=(kc == NC - 1))
                        tch.inc(mm)

                    T(t_sums)

                    def s_sq(s):
                        self.W(s, "s", tch)
                        for kc in range(NC):
                            sop(s, lambda kc=kc: s.activation(
                                xr[:, kc, :], xr[:, kc, :], ACT.Square))

                    S(s_sq)
                    sq_tok = sch.n

                    def t_sumsq(t):
                        self.W(t, "t", sch, sq_tok)
                        for kc in range(NC):
                            mm = t.matmul(sumsq[:, :], onesr[:, :], xr[:, kc, :],
                                          start=(kc == 0), stop=(kc == NC - 1))
                        tch.inc(mm)

                    T(t_sumsq)
                    sumsq_tok[0] = tch.n

                def a_smalls(i):
                    stats_tok = tch.n

                    def v_ln1(v):
                        self.W(v, "v", tch, stats_tok)
                        vop(v, lambda: v.tensor_scalar(mbA[:, :], sums[:, :], 1.0 / D, None, AL.mult))
                        vop(v, lambda: v.tensor_tensor(t1A[:, :], mbA[:, :], sums[:, :], AL.mult))
                        vop(v, lambda: v.tensor_tensor(t2A[:, :], sumsq[:, :], t1A[:, :], AL.subtract))

                    V(v_ln1)
                    ln1_tok = vch.n
                    stats_free[0] = vch.n

                    def s_sd(s):
                        self.W(s, "s", vch, ln1_tok)
                        sop(s, lambda: s.activation(t2A[:, :], t2A[:, :], ACT.Ln,
                                                    bias=cvec[:, 0:1], scale=cvec[:, 1:2]))
                        sop(s, lambda: s.activation(riA[:, :], t2A[:, :], ACT.Exp,
                                                    bias=0.0, scale=cvec[:, 2:3]))

                    S(s_sd)
                    return sch.n

                def a_apply(i, sd_tok, kc0, kc1):
                    def v_apply(v, i=i):
                        self.W(v, "v", swv, 80)
                        self.W(v, "v", sch, sd_tok)
                        for kc in range(kc0, kc1):
                            vop(v, lambda kc=kc: v.scalar_tensor_tensor(
                                t1A[:, :], xtlA[:, i % 2, kc, :], 1.0, mbA[:, :],
                                AL.mult, AL.subtract))
                            vop(v, lambda kc=kc: v.tensor_tensor(
                                t2A[:, :], t1A[:, :], riA[:, :], AL.mult))
                            vop(v, lambda kc=kc: v.tensor_scalar(
                                xnbf[:, i % 2, kc, :], t2A[:, :], gsb[:, kc:kc + 1],
                                bsb[:, kc:kc + 1], AL.mult, AL.add))

                    V(v_apply)
                    if kc1 == NC:
                        xn_tok[i] = vch.n
                        xa_free[i] = vch.n

                def a_qunits(i):
                    for ncx in range(NC):
                        slot = ncx % 4

                        def t_q(t, i=i, ncx=ncx, slot=slot):
                            self.W(t, "t", vch, xn_tok[i])
                            self.W(t, "t", swq, 16 * NC)
                            self.W(t, "t", vch, psA_free[slot])
                            for kc in range(NC):
                                mm = t.matmul(psA[slot][:, :],
                                              wqbf[:, kc, ncx * 128:(ncx + 1) * 128],
                                              xnbf[:, i % 2, kc, :],
                                              start=(kc == 0), stop=(kc == NC - 1))
                            tch.inc(mm)

                        T(t_q)
                        q_tok = tch.n
                        eb = exctr[0] % 3
                        exctr[0] += 1

                        def s_exp(s, slot=slot, eb=eb):
                            self.W(s, "s", tch, q_tok)
                            if exfree[eb]:
                                self.W(s, "s", vch, exfree[eb])
                            sop(s, lambda: s.activation(exA[:, eb, :], psA[slot][:, :],
                                                        ACT.Exp))

                        S(s_exp)
                        ex_tok = sch.n

                        def v_q(v, i=i, ncx=ncx, slot=slot, eb=eb):
                            self.W(v, "v", sch, ex_tok)
                            vop(v, lambda: v.tensor_scalar(
                                exA[:, eb, :], exA[:, eb, :], 1.0, None, AL.min))
                            vop(v, lambda: v.scalar_tensor_tensor(
                                qst[:, ncx, i * TT:(i + 1) * TT], psA[slot][:, :], 1.0,
                                exA[:, eb, :], AL.add, AL.max))

                        V(v_q)
                        psA_free[slot] = vch.n
                        exfree[eb] = vch.n

                def a_kvgroup(i, g_):
                    ngr = TT // 128
                    gbuf = (i * ngr + g_) % 2
                    for piece in range(4):
                        slot = piece % 4

                        def t_kv(t, i=i, g_=g_, piece=piece, slot=slot, gbuf=gbuf):
                            self.W(t, "t", vch, xn_tok[i])
                            self.W(t, "t", vch, psA_free[slot])
                            if piece == 0:
                                self.W(t, "t", tch, kbf_free[gbuf])
                            for kc in range(NC):
                                mm = t.matmul(psA[slot][:, :],
                                              xnbf[:, i % 2, kc, g_ * 128:(g_ + 1) * 128],
                                              wqkv_rhs(kc, piece),
                                              start=(kc == 0), stop=(kc == NC - 1))
                            tch.inc(mm)

                        def wqkv_rhs(kc, piece):
                            return wqbf[:, kc, D + piece * 512:D + (piece + 1) * 512]

                        T(t_kv)
                        kv_tok = tch.n

                        if piece < 2:
                            eb = exctr[0] % 3
                            exctr[0] += 1

                            def s_exp2(s, slot=slot, eb=eb):
                                self.W(s, "s", tch, kv_tok)
                                if exfree[eb]:
                                    self.W(s, "s", vch, exfree[eb])
                                sop(s, lambda: s.activation(exA[:, eb, :],
                                                            psA[slot][:, :], ACT.Exp))

                            S(s_exp2)
                            ex2 = sch.n

                            def v_k(v, piece=piece, slot=slot, gbuf=gbuf, eb=eb):
                                self.W(v, "v", sch, ex2)
                                vop(v, lambda: v.tensor_scalar(
                                    exA[:, eb, :], exA[:, eb, :], 1.0, None, AL.min))
                                vop(v, lambda: v.scalar_tensor_tensor(
                                    kbf[:, gbuf, piece * 512:(piece + 1) * 512],
                                    psA[slot][:, :], 1.0, exA[:, eb, :], AL.add, AL.max))

                            V(v_k)
                            psA_free[slot] = vch.n
                            exfree[eb] = vch.n
                        else:
                            def v_v(v, piece=piece, slot=slot, gbuf=gbuf):
                                self.W(v, "v", tch, kv_tok)
                                p0 = (piece - 2) * 8
                                vop(v, lambda: v.tensor_copy(
                                    v65[:, gbuf, p0:p0 + 8, 0:64],
                                    psA[slot][:, :].rearrange("p (h j) -> p h j", j=64)))

                            V(v_v)
                            psA_free[slot] = vch.n

                    evac_tok = vch.n

                    def kvmm_emit(gbuf=gbuf, evac_tok=evac_tok):
                        def t_kvmm(t, gbuf=gbuf):
                            self.W(t, "t", vch, evac_tok)
                            self.W(t, "t", vch, kvacc_free[0])
                            for c in range(NC):
                                tgt = kvacc0 if c < 4 else kvacc1
                                cc = c % 4
                                for half in range(2):
                                    h = 2 * c + half
                                    mm = t.matmul(tgt[half * 64:(half + 1) * 64,
                                                      cc * 72:cc * 72 + 65],
                                                  kbf[:, gbuf, h * 64:(h + 1) * 64],
                                                  v65[:, gbuf, h, :],
                                                  start=True, stop=True)
                            tch.inc(mm)

                        T(t_kvmm)
                        kbf_free[gbuf] = tch.n
                        kvmm_tok = tch.n

                        def v_kvacc(v, kvmm_tok=kvmm_tok):
                            self.W(v, "v", tch, kvmm_tok)
                            for half2 in range(2):
                                tgt = kvacc0 if half2 == 0 else kvacc1
                                vop(v, lambda tgt=tgt, half2=half2: v.tensor_add(
                                    kvsum[:, half2 * 4:(half2 + 1) * 4, :],
                                    kvsum[:, half2 * 4:(half2 + 1) * 4, :],
                                    tgt[:, :].rearrange("p (c e) -> p c e", e=72)[:, :, 0:65]))

                        V(v_kvacc)
                        kvacc_free[0] = vch.n
                    return kvmm_emit

                a_load(0)
                a_load(1)
                a_stats(0)
                sd0 = a_smalls(0)
                a_apply(0, sd0, 0, NC)
                pend = [None]
                for i in range(NT):
                    a_qunits(i)
                    if i + 1 < NT:
                        a_stats(i + 1)
                        sdn = a_smalls(i + 1)
                    for g_ in range(4):
                        ke = a_kvgroup(i, g_)
                        if pend[0] is not None:
                            pend[0]()
                        pend[0] = ke
                        if g_ == 0 and i + 1 < NT:
                            a_apply(i + 1, sdn, 0, NC // 2)
                        if g_ == 1 and i + 1 < NT:
                            a_apply(i + 1, sdn, NC // 2, NC)
                            a_load(i + 2)
                if pend[0] is not None:
                    pend[0]()

                def v_kvev(v):
                    vop(v, lambda: v.tensor_copy(kvks[:, :, :], kvsum[:, :, :]))
                    vop(v, lambda: v.memset(bdg[:, :, :], 0.0))
                    for h in range(16):
                        half = h % 2
                        c = h // 2
                        vop(v, lambda h=h, half=half, c=c: v.tensor_copy(
                            bdg[half * 64:(half + 1) * 64, c, h:h + 1],
                            kvks[half * 64:(half + 1) * 64, c, 64:65]))

                V(v_kvev)

            self._barrier(pb, 5)

            esB = contextlib.ExitStack()
            with esB:
                wor = esB.enter_context(nc.sbuf_tensor("wor", [128, NC, D], F32R))
                xtlB = esB.enter_context(nc.sbuf_tensor("xtlB", [128, 2, NC, TT], F32))
                atr = esB.enter_context(nc.sbuf_tensor("atr", [128, NC, TT], F32R))
                ysb = esB.enter_context(nc.sbuf_tensor("ysb", [128, NC, TT], F32))
                x2tl = esB.enter_context(nc.sbuf_tensor("x2tl", [128, 1, NC, TT], F32))
                yr = atr
                rcd = esB.enter_context(nc.sbuf_tensor("rcd", [16, TT], F32))
                rcdr = esB.enter_context(nc.sbuf_tensor("rcdr", [16, 2, TT], F32R))
                mbB = esB.enter_context(nc.sbuf_tensor("mbB", [128, TT], F32))
                t1B = esB.enter_context(nc.sbuf_tensor("t1B", [128, 2, TT], F32))
                t2B = esB.enter_context(nc.sbuf_tensor("t2B", [128, TT], F32))
                riB = esB.enter_context(nc.sbuf_tensor("riB", [128, TT], F32))
                psB0 = esB.enter_context(nc.psum_tensor("psB0", [128, TT], F32))
                psB1 = esB.enter_context(nc.psum_tensor("psB1", [128, TT], F32))
                psB2 = esB.enter_context(nc.psum_tensor("psB2", [128, TT], F32))
                psB3 = esB.enter_context(nc.psum_tensor("psB3", [128, TT], F32))
                psB4 = esB.enter_context(nc.psum_tensor("psB4", [128, TT], F32))
                denp = esB.enter_context(nc.psum_tensor("denp", [16, TT], F32))
                sumsB = esB.enter_context(nc.psum_tensor("sumsB", [128, TT], F32))
                sumsqB = esB.enter_context(nc.psum_tensor("sumsqB", [128, TT], F32))
                psB = [psB0, psB1, psB2, psB3, psB4]
                psB_free = [(vch, 0)] * 5
                statsB_free = [(vch, 0)]
                denp_free = [(sch, 0)]
                t1f = [(vch, 0), (vch, 0)]
                xb_free = [0] * (NT + 2)
                rcd_tok = [0] * (NT + 2)
                stats_tok = [0] * (NT + 1)
                y_toks = [[0] * NC for _ in range(NT)]

                def g_wo(g):
                    for ec in range(NC):
                        g.dma_start(out=wor[:, ec, :],
                                    in_=wout.ap()[ec * 128:(ec + 1) * 128, :]).then_inc(s_wo, 16)
                        swo.n += 16

                G(g_wo)

                def b_load(i):
                    if i >= NT:
                        return

                    def y_loadB(y, i=i):
                        if i >= 2:
                            self.W(y, "y", vch, xb_free[i - 2])
                        dq_xb.issue(y, "y", i,
                                    y.dma_start(out=xtlB[:, i % 2, :, :], in_=xT_t[i]))

                    Y(y_loadB)

                def b_den(i):
                    if i >= NT:
                        return

                    def t_den(t, i=i):
                        ch0, v0 = denp_free[0]
                        if v0:
                            self.W(t, "t", ch0, v0)
                        for kc in range(NC):
                            mm = t.matmul(denp[:, :], bdg[:, kc, :],
                                          qst[:, kc, i * TT:(i + 1) * TT],
                                          start=(kc == 0), stop=(kc == NC - 1))
                        tch.inc(mm)

                    T(t_den)
                    den_tok = tch.n

                    def s_recip(s, i=i, den_tok=den_tok):
                        self.W(s, "s", tch, den_tok)
                        sop(s, lambda: s.activation(rcd[:, :], denp[:, :], ACT.Ln,
                                                    bias=cvec[0:16, 4:5], scale=1.0))
                        sop(s, lambda: s.activation(rcdr[:, i % 2, :], rcd[:, :],
                                                    ACT.Exp, bias=0.0,
                                                    scale=cvec[0:16, 3:4]))

                    S(s_recip)
                    rcd_tok[i] = sch.n
                    denp_free[0] = (sch, sch.n)

                def b_pairs(i):
                    for c in range(NC):
                        s0 = c % 2
                        s1 = 2 + (c % 2)

                        def t_pair(t, i=i, c=c, s0=s0, s1=s1):
                            ch0, v0 = psB_free[s0]
                            if v0:
                                self.W(t, "t", ch0, v0)
                            for half in range(2):
                                mm = t.matmul(psB[s0][half * 64:(half + 1) * 64, :],
                                              kvks[half * 64:(half + 1) * 64, c, 0:64],
                                              qst[half * 64:(half + 1) * 64, c,
                                                  i * TT:(i + 1) * TT],
                                              start=True, stop=True)
                                tch.inc(mm)
                            self.W(t, "t", swv, 80)
                            self.W(t, "t", sch, rcd_tok[i])
                            ch1, v1 = psB_free[s1]
                            if v1:
                                self.W(t, "t", ch1, v1)
                            mm = t.matmul(psB[s1][:, :],
                                          selE[:, c * 128:(c + 1) * 128],
                                          rcdr[:, i % 2, :], start=True, stop=True)
                            tch.inc(mm)

                        T(t_pair)
                        pair_tok = tch.n

                        def s_dc(s, c=c, s1=s1, pair_tok=pair_tok):
                            self.W(s, "s", tch, pair_tok)
                            ch2, v2 = t1f[c % 2]
                            if v2:
                                self.W(s, "s", ch2, v2)
                            sop(s, lambda: s.activation(t1B[:, c % 2, :],
                                                        psB[s1][:, :], ACT.Copy))

                        S(s_dc)
                        dc_tok = sch.n
                        psB_free[s1] = (sch, dc_tok)

                        def v_div(v, c=c, s0=s0, dc_tok=dc_tok):
                            self.W(v, "v", sch, dc_tok)
                            vop(v, lambda: v.tensor_tensor(
                                atr[:, c, :], psB[s0][:, :], t1B[:, c % 2, :],
                                AL.mult))

                        V(v_div)
                        psB_free[s0] = (vch, vch.n)
                        t1f[c % 2] = (vch, vch.n)

                def b_wout(i, at_tok):
                    wout_last = [0]
                    for fc in range(NC):
                        slot = fc % 5

                        def t_wout(t, fc=fc, slot=slot):
                            self.W(t, "t", vch, at_tok)
                            self.W(t, "t", swo, 16 * NC)
                            ch0, v0 = psB_free[slot]
                            if v0:
                                self.W(t, "t", ch0, v0)
                            for ec in range(NC):
                                mm = t.matmul(psB[slot][:, :],
                                              wor[:, ec, fc * 128:(fc + 1) * 128],
                                              atr[:, ec, :],
                                              start=(ec == 0), stop=(ec == NC - 1))
                            tch.inc(mm)

                        T(t_wout)
                        wout_tok = tch.n
                        wout_last[0] = wout_tok

                        def v_y(v, i=i, fc=fc, slot=slot):
                            self.W(v, "v", tch, wout_tok)
                            dq_xb.wait(v, "v", i)
                            vop(v, lambda: v.scalar_tensor_tensor(
                                ysb[:, fc, :], xtlB[:, i % 2, fc, :], 2.0,
                                psB[slot][:, :], AL.mult, AL.add))

                        V(v_y)
                        psB_free[slot] = (vch, vch.n)
                        y_toks[i][fc] = vch.n
                    xb_free[i] = vch.n
                    return wout_last[0]

                def b_stats(i, wout_all):
                    yr_toks = []
                    sums_toks = []
                    ysq_toks = []

                    def s_yr(s, i=i):
                        self.W(s, "s", tch, wout_all)
                        for kc in range(NC):
                            self.W(s, "s", vch, y_toks[i][kc])
                            sop(s, lambda kc=kc: s.activation(
                                yr[:, kc, :], ysb[:, kc, :], ACT.Copy))
                            yr_toks.append(sch.n)

                    S(s_yr)

                    def t_sumsB(t):
                        ch0, v0 = statsB_free[0]
                        if v0:
                            self.W(t, "t", ch0, v0)
                        for kc in range(NC):
                            self.W(t, "t", sch, yr_toks[kc])
                            mm = t.matmul(sumsB[:, :], onesr[:, :], yr[:, kc, :],
                                          start=(kc == 0), stop=(kc == NC - 1))
                            tch.inc(mm)
                            sums_toks.append(tch.n)

                    T(t_sumsB)

                    def s_ysq(s):
                        for kc in range(NC):
                            self.W(s, "s", tch, sums_toks[kc])
                            sop(s, lambda kc=kc: s.activation(
                                yr[:, kc, :], yr[:, kc, :], ACT.Square))
                            ysq_toks.append(sch.n)

                    S(s_ysq)

                    def t_sumsqB(t):
                        for kc in range(NC):
                            self.W(t, "t", sch, ysq_toks[kc])
                            mm = t.matmul(sumsqB[:, :], onesr[:, :], yr[:, kc, :],
                                          start=(kc == 0), stop=(kc == NC - 1))
                            tch.inc(mm)

                    T(t_sumsqB)
                    stats_tok[i] = tch.n

                def b_tail(i):
                    if i < 0:
                        return

                    def v_ln2(v, i=i):
                        self.W(v, "v", tch, stats_tok[i])
                        vop(v, lambda: v.tensor_scalar(mbB[:, :], sumsB[:, :], 1.0 / D, None, AL.mult))
                        vop(v, lambda: v.tensor_tensor(riB[:, :], mbB[:, :], sumsB[:, :], AL.mult))
                        vop(v, lambda: v.tensor_tensor(t2B[:, :], sumsqB[:, :], riB[:, :], AL.subtract))

                    V(v_ln2)
                    ln2_tok = vch.n
                    statsB_free[0] = (vch, vch.n)

                    def s_sdB(s):
                        self.W(s, "s", vch, ln2_tok)
                        sop(s, lambda: s.activation(t2B[:, :], t2B[:, :], ACT.Ln,
                                                    bias=cvec[:, 0:1], scale=cvec[:, 1:2]))
                        sop(s, lambda: s.activation(riB[:, :], t2B[:, :], ACT.Exp,
                                                    bias=0.0, scale=cvec[:, 2:3]))

                    S(s_sdB)
                    sdB_tok = sch.n

                    def v_x2(v, i=i):
                        self.W(v, "v", sch, sdB_tok)
                        dq_x2w.wait(v, "v", i - 1)
                        for kc in range(NC):
                            vop(v, lambda kc=kc: v.scalar_tensor_tensor(
                                t2B[:, :], ysb[:, kc, :], 1.0, mbB[:, :],
                                AL.mult, AL.subtract))
                            vop(v, lambda kc=kc: v.tensor_tensor(
                                x2tl[:, 0, kc, :], t2B[:, :], riB[:, :], AL.mult))

                    V(v_x2)
                    x2_tok = vch.n

                    def y_x2w(y, i=i):
                        self.W(y, "y", vch, x2_tok)
                        dq_x2w.issue(y, "y", i, y.dma_start(out=x2d_t[i], in_=x2tl[:, 0, :, :]))

                    Y(y_x2w)

                    def v_x2r(v, i=i):
                        vop(v, lambda: v.tensor_reduce(
                            mxac[:, i:i + 1],
                            x2tl[:, 0, :, :].rearrange("p c t -> p (c t)"),
                            AX, AL.max, apply_absolute_value=True))

                    V(v_x2r)

                b_load(0)
                b_load(1)
                b_den(0)
                b_den(1)
                for i in range(NT):
                    b_pairs(i)
                    div_tok = vch.n
                    b_tail(i - 1)
                    wout_all = b_wout(i, div_tok)
                    b_stats(i, wout_all)
                    b_den(i + 2)
                    b_load(i + 2)
                b_tail(NT - 1)

                def v_mx(v):
                    vop(v, lambda: v.tensor_reduce(mxac[:, NT * NC:NT * NC + 1],
                                                   mxac[:, 0:NT], AX, AL.max))

                V(v_mx)
                mx_tok = vch.n

                def g_flip(g):
                    self.W(g, "g", vch, mx_tok)
                    g.dma_start(out=redd.ap().rearrange("(p o) -> p o", o=1),
                                in_=mxac[:, NT * NC:NT * NC + 1]).then_inc(s_gm, 16)
                    sgm.n += 16
                    g.wait_ge(s_gm, sgm.n)
                    g.dma_start(out=mxrow[:, :],
                                in_=redd.ap().rearrange("(o p) -> o p", o=1)).then_inc(s_gm, 16)
                    sgm.n += 16

                G(g_flip)
                flip_tok = sgm.n

                def v_gmax(v):
                    self.W(v, "v", sgm, flip_tok)
                    vop(v, lambda: v.tensor_reduce(mx1[0:1, 0:1], mxrow[:, :], AX, AL.max))

                V(v_gmax)
                gmax_tok = vch.n

                def g_cc(g):
                    self.W(g, "g", vch, gmax_tok)
                    g.dma_start(out=cc_in[:, :], in_=mx1[:, :]).then_inc(s_gm, 16)
                    sgm.n += 16
                    g.wait_ge(s_gm, sgm.n)
                    g.collective_compute(
                        "AllReduce", AL.max,
                        replica_groups=[list(range(self.n_cores))],
                        ins=[cc_in.ap().opt()], outs=[cc_out.ap().opt()],
                    ).then_inc(c_cc, 1)
                    g.wait_ge(c_cc, 1)
                    g.dma_start(out=mx1[:, :], in_=cc_out[:, :]).then_inc(s_gm, 16)
                    sgm.n += 16

                G(g_cc)
                cc_tok = sgm.n

                def v_xsc(v):
                    self.W(v, "v", sgm, cc_tok)
                    vop(v, lambda: v.tensor_scalar(mx1[0:1, 1:2], mx1[0:1, 0:1],
                                                   1.0 / 127.0, Q_EPS, AL.mult, AL.max))
                    vop(v, lambda: v.reciprocal(mx1[0:1, 2:3], mx1[0:1, 1:2]))

                V(v_xsc)
                xsc_tok = vch.n

                def t_xscb(t):
                    self.W(t, "t", vch, xsc_tok)
                    mm = t.matmul(sumsB[:, 0:2], ones[0:1, :],
                                  mx1[0:1, 1:3], start=True, stop=True)
                    tch.inc(mm)

                T(t_xscb)
                xscb_tok = tch.n

                def v_xsc2(v):
                    self.W(v, "v", tch, xscb_tok)
                    vop(v, lambda: v.tensor_copy(xscq[:, :], sumsB[:, 0:2]))
                    self.W(v, "v", swv, 80)
                    vop(v, lambda: v.tensor_scalar(swsc[:, :], wssb[:, :],
                                                   xscq[:, 0:1], None, AL.mult))

                V(v_xsc2)

            esQST.close()

            self._barrier(pb, 10, dqs=(dq_x2w,))

            esC = contextlib.ExitStack()
            with esC:
                wibf = esC.enter_context(nc.sbuf_tensor("wibf", [128, NC, D], BF16))
                x2tC = esC.enter_context(nc.sbuf_tensor("x2tC", [128, 2, NC, TT], F32))
                qsc = esC.enter_context(nc.sbuf_tensor("qsc", [128, NC, TT], F32))
                wibf_ = wibf

                def g_wi(g):
                    for kc in range(NC):
                        g.dma_start(out=wibf_[:, kc, :],
                                    in_=wi8T.ap()[kc * 128:(kc + 1) * 128, :]).then_inc(s_wi, 16)
                        swi.n += 16

                G(g_wi)
                xqbf = esC.enter_context(nc.sbuf_tensor("xqbf", [128, 2, NC, TT], BF16))
                y3 = esC.enter_context(nc.sbuf_tensor("y3", [128, 2, NC, TT], F32))
                y3r = esC.enter_context(nc.sbuf_tensor("y3r", [128, NC, TT], F32R))
                y3sq = esC.enter_context(nc.sbuf_tensor("y3sq", [128, NC, TT], F32R))
                otl = esC.enter_context(nc.sbuf_tensor("otl", [128, 2, NC, TT], F32))
                mbC = esC.enter_context(nc.sbuf_tensor("mbC", [128, TT], F32))
                t1C = esC.enter_context(nc.sbuf_tensor("t1C", [128, TT], F32))
                t2C = esC.enter_context(nc.sbuf_tensor("t2C", [128, TT], F32))
                riC = esC.enter_context(nc.sbuf_tensor("riC", [128, TT], F32))
                psC = [esC.enter_context(nc.psum_tensor("psC%d" % k, [128, TT], F32))
                       for k in range(6)]
                sumsC = esC.enter_context(nc.psum_tensor("sumsC", [128, TT], F32))
                sumsqC = esC.enter_context(nc.psum_tensor("sumsqC", [128, TT], F32))
                psC_free = [0] * 6
                statsC_free = [(vch, 0)]
                x2c_free = [0] * (NT + 2)
                q_tok = [0] * (NT + 1)
                statsC_tok = [0] * (NT + 1)
                epi_toks = [[0] * NC for _ in range(NT)]
                y3r_toks = [[0] * NC for _ in range(NT)]
                sumsT_toks = [[0] * NC for _ in range(NT)]
                sumsqT_toks = [[0] * NC for _ in range(NT)]
                sdC_toks = [0] * NT
                sq_toks = [[0] * NC for _ in range(NT)]
                vq_toks = [[0] * NC for _ in range(NT)]
                qs_toks = [[0] * NC for _ in range(NT)]

                def c_load(i):
                    if i >= NT:
                        return

                    def y_loadC(y, i=i):
                        dq_x2w.wait(y, "y", i)
                        if i >= 2:
                            self.W(y, "y", vch, x2c_free[i - 2])
                        dq_x2r.issue(y, "y", i,
                                     y.dma_start(out=x2tC[:, i % 2, :, :], in_=x2d_t[i]))

                    Y(y_loadC)

                def c_quant_v(i):
                    if i >= NT:
                        return

                    def v_q(v, i=i):
                        dq_x2r.wait(v, "v", i)
                        for kc in range(NC):
                            if i >= 1 and qs_toks[i - 1][kc]:
                                self.W(v, "v", sch, qs_toks[i - 1][kc])
                            vop(v, lambda kc=kc: v.tensor_scalar(
                                qsc[:, kc, :], x2tC[:, i % 2, kc, :],
                                xscq[:, 1:2], MAGIC, AL.mult, AL.add))
                            vq_toks[i][kc] = vch.n

                    V(v_q)

                def c_quant_s(i):
                    if i >= NT:
                        return

                    def s_q(s, i=i):
                        for kc in range(NC):
                            self.W(s, "s", vch, vq_toks[i][kc])
                            sop(s, lambda kc=kc: s.activation(
                                xqbf[:, i % 2, kc, :], qsc[:, kc, :],
                                ACT.Identity, bias=cvec[:, 5:6], scale=1.0))
                            qs_toks[i][kc] = sch.n

                    S(s_q)
                    q_tok[i] = sch.n

                def c_ffn(i):
                    for ncx in range(NC):
                        slot = ncx % 6

                        def t_ffn(t, i=i, ncx=ncx, slot=slot):
                            self.W(t, "t", sch, q_tok[i])
                            self.W(t, "t", swi, 16 * NC)
                            self.W(t, "t", vch, psC_free[slot])
                            for kc in range(NC):
                                mm = t.matmul(psC[slot][:, :],
                                              wibf[:, kc, ncx * 128:(ncx + 1) * 128],
                                              xqbf[:, i % 2, kc, :],
                                              start=(kc == 0), stop=(kc == NC - 1))
                            tch.inc(mm)

                        T(t_ffn)
                        ffn_tok = tch.n

                        def v_epi(v, i=i, ncx=ncx, slot=slot):
                            self.W(v, "v", tch, ffn_tok)
                            if i >= 2 and y3r_toks[i - 2][ncx]:
                                self.W(v, "v", sch, y3r_toks[i - 2][ncx])
                            vop(v, lambda: v.scalar_tensor_tensor(
                                y3[:, i % 2, ncx, :], psC[slot][:, :],
                                swsc[:, ncx:ncx + 1],
                                x2tC[:, i % 2, ncx, :], AL.mult, AL.add))

                        V(v_epi)
                        psC_free[slot] = vch.n
                        epi_toks[i][ncx] = vch.n
                    x2c_free[i] = vch.n

                def c_stats_s(i):
                    def s_y3r(s, i=i):
                        for kc in range(NC):
                            self.W(s, "s", vch, epi_toks[i][kc])
                            if i >= 1 and sumsT_toks[i - 1][kc]:
                                self.W(s, "s", tch, sumsT_toks[i - 1][kc])
                            sop(s, lambda kc=kc: s.activation(
                                y3r[:, kc, :], y3[:, i % 2, kc, :],
                                ACT.Identity, bias=fbsb[:, kc:kc + 1], scale=1.0))
                            y3r_toks[i][kc] = sch.n
                            if i >= 1 and sumsqT_toks[i - 1][kc]:
                                self.W(s, "s", tch, sumsqT_toks[i - 1][kc])
                            sop(s, lambda kc=kc: s.activation(
                                y3sq[:, kc, :], y3r[:, kc, :], ACT.Square))
                            sq_toks[i][kc] = sch.n

                    S(s_y3r)

                def c_stats_t(i):
                    if i < 0:
                        return

                    def t_sumsC(t, i=i):
                        ch0, v0 = statsC_free[0]
                        if v0:
                            self.W(t, "t", ch0, v0)
                        for kc in range(NC):
                            self.W(t, "t", sch, y3r_toks[i][kc])
                            mm = t.matmul(sumsC[:, :], onesr[:, :], y3r[:, kc, :],
                                          start=(kc == 0), stop=(kc == NC - 1))
                            tch.inc(mm)
                            sumsT_toks[i][kc] = tch.n
                        for kc in range(NC):
                            self.W(t, "t", sch, sq_toks[i][kc])
                            mm = t.matmul(sumsqC[:, :], onesr[:, :], y3sq[:, kc, :],
                                          start=(kc == 0), stop=(kc == NC - 1))
                            tch.inc(mm)
                            sumsqT_toks[i][kc] = tch.n

                    T(t_sumsC)
                    statsC_tok[i] = tch.n

                def c_tail_ln(i):
                    if i < 0:
                        return

                    def v_ln3(v, i=i):
                        self.W(v, "v", tch, statsC_tok[i])
                        vop(v, lambda: v.tensor_scalar(mbC[:, :], sumsC[:, :], 1.0 / D, None, AL.mult))
                        vop(v, lambda: v.tensor_tensor(t1C[:, :], mbC[:, :], sumsC[:, :], AL.mult))
                        vop(v, lambda: v.tensor_tensor(t2C[:, :], sumsqC[:, :], t1C[:, :], AL.subtract))

                    V(v_ln3)
                    ln3_tok = vch.n
                    statsC_free[0] = (vch, vch.n)

                    def s_sdC(s):
                        self.W(s, "s", vch, ln3_tok)
                        sop(s, lambda: s.activation(t2C[:, :], t2C[:, :], ACT.Ln,
                                                    bias=cvec[:, 0:1], scale=cvec[:, 1:2]))
                        sop(s, lambda: s.activation(riC[:, :], t2C[:, :], ACT.Exp,
                                                    bias=0.0, scale=cvec[:, 2:3]))

                    S(s_sdC)
                    sdC_toks[i] = sch.n

                def c_tail_out(i):
                    if i < 0:
                        return
                    sdC_tok = sdC_toks[i]

                    def v_out(v, i=i):
                        self.W(v, "v", sch, sdC_tok)
                        dq_out.wait(v, "v", i - 2)
                        for kc in range(NC):
                            vop(v, lambda kc=kc, i=i: v.scalar_tensor_tensor(
                                t1C[:, :], y3[:, i % 2, kc, :], fbsb[:, kc:kc + 1],
                                mbC[:, :], AL.add, AL.subtract))
                            vop(v, lambda kc=kc, i=i: v.tensor_tensor(
                                otl[:, i % 2, kc, :], t1C[:, :], riC[:, :], AL.mult))

                    V(v_out)
                    out_tok = vch.n

                    def y_store(y, i=i):
                        self.W(y, "y", vch, out_tok)
                        dq_out.issue(y, "y", i, y.dma_start(out=outT_t[i], in_=otl[:, i % 2, :, :]))

                    Y(y_store)

                c_load(0)
                c_load(1)
                c_quant_v(0)
                c_quant_s(0)
                for i in range(NT):
                    c_ffn(i)
                    c_quant_v(i + 1)
                    c_quant_s(i + 1)
                    c_stats_t(i - 1)
                    c_tail_ln(i - 1)
                    c_stats_s(i)
                    c_tail_out(i - 1)
                    c_load(i + 2)
                c_stats_t(NT - 1)
                c_tail_ln(NT - 1)
                c_tail_out(NT - 1)

            def g_fin(g):
                for dq in (dq_out, dq_xa, dq_xb, dq_x2w, dq_x2r):
                    for sem, n in dq.total():
                        if n:
                            g.wait_ge(sem, n)
                for ch in (swv, swq, swo, swi, sgm):
                    if ch.n:
                        g.wait_ge(ch.sem, ch.n)

            G(g_fin)

        return nc

    def _barrier(self, pb, target, dqs=()):
        chains = {"t": self.tch, "v": self.vch, "s": self.sch}
        for name, sec in (("t", self.block.tensor), ("v", self.block.vector),
                          ("s", self.block.scalar), ("g", self.block.gpsimd),
                          ("y", self.block.sync)):
            def f(eng, pb=pb, name=name):
                ch = chains.get(name)
                if ch is not None:
                    self.W(eng, name, ch)
                if name == "y":
                    for dq in dqs:
                        for sem, n in dq.total():
                            if n:
                                eng.wait_ge(sem, n)
                eng.nop().then_inc(pb.sem, 1)
            sec(f)
        pb.n += 5

        for name, sec in (("t", self.block.tensor), ("v", self.block.vector),
                          ("s", self.block.scalar), ("g", self.block.gpsimd),
                          ("y", self.block.sync)):
            def f(eng, pb=pb, target=target):
                eng.wait_ge(pb.sem, target)
            sec(f)


def build_nc(L=4096, n_cores=8):
    return Builder(L, n_cores).build()


_NC_CACHE = {}


def kernel(L0_attn_x, L0_attn_gamma, L0_attn_beta, L0_attn_Wqkv, L0_attn_Wout,
           L0_ffn_weight_int8, L0_ffn_weight_scale, L0_ffn_bias):
    x = np.asarray(L0_attn_x)
    B, L, Dm = x.shape
    assert Dm == D
    n_cores = 8
    key = (L, n_cores)
    if key not in _NC_CACHE:
        _NC_CACHE[key] = build_nc(L=L, n_cores=n_cores)
    nc = _NC_CACHE[key]

    wqkv = np.ascontiguousarray(np.asarray(L0_attn_Wqkv, dtype=np.float32))
    wout = np.ascontiguousarray(np.asarray(L0_attn_Wout, dtype=np.float32))
    wi8T = np.ascontiguousarray(np.asarray(L0_ffn_weight_int8).T)
    gam = np.ascontiguousarray(np.asarray(L0_attn_gamma, dtype=np.float32))
    bet = np.ascontiguousarray(np.asarray(L0_attn_beta, dtype=np.float32))
    fbias = np.ascontiguousarray(np.asarray(L0_ffn_bias, dtype=np.float32))
    wsc = np.ascontiguousarray(np.asarray(L0_ffn_weight_scale, dtype=np.float32).reshape(-1))

    selc = np.zeros((16, D), dtype=np.float32)
    for c in range(NC):
        selc[2 * c, c * 128:c * 128 + 64] = 1.0
        selc[2 * c + 1, c * 128 + 64:(c + 1) * 128] = 1.0

    in_maps = []
    for c in range(n_cores):
        in_maps.append({
            "xT": np.ascontiguousarray(x[c].T.astype(np.float32, copy=False)),
            "wqkv": wqkv, "wout": wout, "wi8T": wi8T,
            "gam": gam, "bet": bet, "fbias": fbias, "wsc": wsc, "selc": selc,
        })
    res = run_bass_kernel_spmd(nc, in_maps, core_ids=list(range(n_cores)))
    out = np.stack([np.ascontiguousarray(r["outT"].T) for r in res.results], axis=0)
    return out.astype(np.float32, copy=False)


# version 23
# speedup vs baseline: 1.1142x; 1.0855x over previous
import contextlib

import numpy as np

import concourse.bass as bass
import concourse.mybir as mybir
from concourse.bass_utils import run_bass_kernel_spmd
from concourse.hw_specs import get_activation_tables

F32 = mybir.dt.float32
F32R = mybir.dt.float32r
BF16 = mybir.dt.bfloat16
I8 = mybir.dt.int8
AL = mybir.AluOpType
ACT = mybir.ActivationFunctionType
AX = mybir.AxisListType.X

D = 1024
NC = 8
H = 16
HD = 64
TT = 512
LN_EPS = 1e-5
DENOM_EPS = 1e-6
Q_EPS = 1e-5
MAGIC = 12582912.0


class Ch:

    def __init__(self, sem):
        self.sem = sem
        self.n = 0

    def inc(self, instr, k=1):
        instr.then_inc(self.sem, k)
        self.n += k
        return self.n


class DQ:

    def __init__(self, bld, sems):
        self.bld = bld
        self.ch = [Ch(sems[0]), Ch(sems[1])]

    def issue(self, eng, ename, i, instr):
        ch = self.ch[i % 2]
        self.bld.W(eng, ename, ch, ch.n)
        instr.then_inc(ch.sem, 16)
        ch.n += 16

    def wait(self, eng, ename, i):
        if i < 0:
            return
        self.bld.W(eng, ename, self.ch[i % 2], 16 * (i // 2 + 1))

    def total(self):
        return [(c.sem, c.n) for c in self.ch]


class Builder:
    def __init__(self, L, n_cores):
        self.L = L
        self.NT = L // TT
        self.n_cores = n_cores
        self.waited = {}

    def W(self, eng, ename, ch, val=None):
        v = ch.n if val is None else val
        if v <= 0:
            return
        key = (ename, ch.sem.num)
        if self.waited.get(key, -1) < v:
            eng.wait_ge(ch.sem, v)
            self.waited[key] = v

    def build(self):
        L, NT = self.L, self.NT
        nc = bass.Bass("TRN2", target_bir_lowering=False, debug=False)
        self.nc = nc

        xT = nc.dram_tensor("xT", [D, L], F32, kind="ExternalInput")
        wqkv = nc.dram_tensor("wqkv", [D, 3 * D], F32, kind="ExternalInput")
        wout = nc.dram_tensor("wout", [D, D], F32, kind="ExternalInput")
        wi8T = nc.dram_tensor("wi8T", [D, D], I8, kind="ExternalInput")
        gam = nc.dram_tensor("gam", [D], F32, kind="ExternalInput")
        bet = nc.dram_tensor("bet", [D], F32, kind="ExternalInput")
        fbias = nc.dram_tensor("fbias", [D], F32, kind="ExternalInput")
        wsc = nc.dram_tensor("wsc", [D], F32, kind="ExternalInput")
        outT = nc.dram_tensor("outT", [D, L], F32, kind="ExternalOutput")
        x2d = nc.dram_tensor("x2d", [NC, 128, L], F32)
        redd = nc.dram_tensor("redd", [128], F32)
        selc = nc.dram_tensor("selc", [16, NC * 128], F32, kind="ExternalInput")
        cc_in = nc.dram_tensor("cc_in", [1, 8], F32)
        cc_out = nc.dram_tensor("cc_out", [1, 8], F32)

        xT_t = xT.ap().rearrange("(c p) (i t) -> i p c t", p=128, t=TT)
        outT_t = outT.ap().rearrange("(c p) (i t) -> i p c t", p=128, t=TT)
        x2d_t = x2d.ap().rearrange("c p (i t) -> i p c t", t=TT)

        act_set_id = None
        for idx, (name, funcs) in enumerate(get_activation_tables(nc.m.arch).items()):
            fs = {str(f).split(".")[-1] for f in funcs}
            if {"Copy", "Square", "Ln", "Exp"} <= fs:
                act_set_id = idx
                break

        es = contextlib.ExitStack()
        with es:
            gsb = es.enter_context(nc.sbuf_tensor("gsb", [128, NC], F32))
            bsb = es.enter_context(nc.sbuf_tensor("bsb", [128, NC], F32))
            fbsb = es.enter_context(nc.sbuf_tensor("fbsb", [128, NC], F32))
            wssb = es.enter_context(nc.sbuf_tensor("wssb", [128, NC], F32))
            swsc = es.enter_context(nc.sbuf_tensor("swsc", [128, NC], F32))
            ones = es.enter_context(nc.sbuf_tensor("ones", [128, 128], F32))
            onesr = es.enter_context(nc.sbuf_tensor("onesr", [128, 128], F32R))
            selE = es.enter_context(nc.sbuf_tensor("selE", [16, NC * 128], F32R))
            mxac = es.enter_context(nc.sbuf_tensor("mxac", [128, NT * NC + 8], F32))
            mxrow = es.enter_context(nc.sbuf_tensor("mxrow", [1, 128], F32))
            mx1 = es.enter_context(nc.sbuf_tensor("mx1", [1, 8], F32))
            xscq = es.enter_context(nc.sbuf_tensor("xscq", [128, 2], F32))
            cvec = es.enter_context(nc.sbuf_tensor("cvec", [128, 6], F32))
            esQST = contextlib.ExitStack()
            qst = esQST.enter_context(nc.sbuf_tensor("qst", [128, NC, L], BF16))
            kvks = esQST.enter_context(nc.sbuf_tensor("kvks", [128, NC, 65], BF16))
            bdg = esQST.enter_context(nc.sbuf_tensor("bdg", [128, NC, 16], BF16))
            s_xa0 = es.enter_context(nc.semaphore("s_xa0"))
            s_xa1 = es.enter_context(nc.semaphore("s_xa1"))
            s_xb0 = es.enter_context(nc.semaphore("s_xb0"))
            s_xb1 = es.enter_context(nc.semaphore("s_xb1"))
            s_w0 = es.enter_context(nc.semaphore("s_w0"))
            s_w1 = es.enter_context(nc.semaphore("s_w1"))
            s_x2r0 = es.enter_context(nc.semaphore("s_x2r0"))
            s_x2r1 = es.enter_context(nc.semaphore("s_x2r1"))
            s_o0 = es.enter_context(nc.semaphore("s_o0"))
            s_o1 = es.enter_context(nc.semaphore("s_o1"))
            s_wv = es.enter_context(nc.semaphore("s_wv"))
            s_wq = es.enter_context(nc.semaphore("s_wq"))
            s_wo = es.enter_context(nc.semaphore("s_wo"))
            s_wi = es.enter_context(nc.semaphore("s_wi"))
            s_gm = es.enter_context(nc.semaphore("s_gm"))
            c_t = es.enter_context(nc.semaphore("c_t"))
            c_v = es.enter_context(nc.semaphore("c_v"))
            c_s = es.enter_context(nc.semaphore("c_s"))
            c_cc = es.enter_context(nc.semaphore("c_cc"))
            pbar = es.enter_context(nc.semaphore("pbar"))
            block = es.enter_context(nc.Block())
            self.block = block
            tch, vch, sch = Ch(c_t), Ch(c_v), Ch(c_s)
            sgm = Ch(s_gm)
            swv, swq, swo, swi = Ch(s_wv), Ch(s_wq), Ch(s_wo), Ch(s_wi)
            dq_xa = DQ(self, (s_xa0, s_xa1))
            dq_xb = DQ(self, (s_xb0, s_xb1))
            dq_x2w = DQ(self, (s_w0, s_w1))
            dq_x2r = DQ(self, (s_x2r0, s_x2r1))
            dq_out = DQ(self, (s_o0, s_o1))
            self.tch, self.vch, self.sch = tch, vch, sch
            pb = Ch(pbar)

            T = block.tensor
            V = block.vector
            S = block.scalar
            G = block.gpsimd
            Y = block.sync

            def vop(v, emit):
                self.W(v, "v", vch)
                vch.inc(emit())

            def sop(s, emit):
                self.W(s, "s", sch)
                sch.inc(emit())

            def g_setup(g):
                g.dma_start(out=selE[:, :], in_=selc.ap()).then_inc(s_wv, 16)
                g.dma_start(out=gsb[:, :], in_=gam.ap().rearrange("(c p) -> p c", p=128),
                            allow_slow_non_contiguous=True).then_inc(s_wv, 16)
                g.dma_start(out=bsb[:, :], in_=bet.ap().rearrange("(c p) -> p c", p=128),
                            allow_slow_non_contiguous=True).then_inc(s_wv, 16)
                g.dma_start(out=fbsb[:, :], in_=fbias.ap().rearrange("(c p) -> p c", p=128),
                            allow_slow_non_contiguous=True).then_inc(s_wv, 16)
                g.dma_start(out=wssb[:, :], in_=wsc.ap().rearrange("(c p) -> p c", p=128),
                            allow_slow_non_contiguous=True).then_inc(s_wv, 16)
                swv.n += 80

            G(g_setup)

            def v_setup(v):
                vop(v, lambda: v.memset(ones[:, :], 1.0))
                vop(v, lambda: v.tensor_copy(onesr[:, :], ones[:, :]))
                vop(v, lambda: v.memset(mx1[:, :], 0.0))
                vop(v, lambda: v.memset(cvec[:, 0:1], LN_EPS))
                vop(v, lambda: v.memset(cvec[:, 1:2], 1.0 / D))
                vop(v, lambda: v.memset(cvec[:, 2:3], -0.5))
                vop(v, lambda: v.memset(cvec[:, 3:4], -1.0))
                vop(v, lambda: v.memset(cvec[:, 4:5], DENOM_EPS))
                vop(v, lambda: v.memset(cvec[:, 5:6], -MAGIC))
                vop(v, lambda: v.memset(mxac[:, :], 0.0))

            V(v_setup)
            setup_tok = vch.n

            if act_set_id is not None:
                def s_pre(s, idx=act_set_id):
                    s.add_instruction(mybir.InstLoadActFuncSet(
                        name=nc.get_next_instruction_name(),
                        act_func_set_id=idx, ins=[], outs=[]))

                S(s_pre)

            esA = contextlib.ExitStack()
            with esA:
                wqbf = esA.enter_context(nc.sbuf_tensor("wqbf", [128, NC, 3 * D], BF16))
                xtlA = esA.enter_context(nc.sbuf_tensor("xtlA", [128, 2, NC, TT], F32))
                xr = esA.enter_context(nc.sbuf_tensor("xr", [128, NC, TT], F32R))
                xnbf = esA.enter_context(nc.sbuf_tensor("xnbf", [128, 2, NC, TT], BF16))
                kbf = esA.enter_context(nc.sbuf_tensor("kbf", [128, 2, 2 * TT], BF16))
                v65 = esA.enter_context(nc.sbuf_tensor("v65", [128, 2, 16, 65], BF16))
                mbA = esA.enter_context(nc.sbuf_tensor("mbA", [128, TT], F32))
                t1A = esA.enter_context(nc.sbuf_tensor("t1A", [128, TT], F32))
                t2A = esA.enter_context(nc.sbuf_tensor("t2A", [128, TT], F32))
                riA = esA.enter_context(nc.sbuf_tensor("riA", [128, TT], F32))
                exA = esA.enter_context(nc.sbuf_tensor("exA", [128, 3, TT], F32))
                psA0 = esA.enter_context(nc.psum_tensor("psA0", [128, TT], F32))
                psA1 = esA.enter_context(nc.psum_tensor("psA1", [128, TT], F32))
                psA2 = esA.enter_context(nc.psum_tensor("psA2", [128, TT], F32))
                psA3 = esA.enter_context(nc.psum_tensor("psA3", [128, TT], F32))
                sums = esA.enter_context(nc.psum_tensor("sums", [128, TT], F32))
                sumsq = esA.enter_context(nc.psum_tensor("sumsq", [128, TT], F32))
                kvacc0 = esA.enter_context(nc.psum_tensor("kvacc0", [128, 288], F32))
                kvacc1 = esA.enter_context(nc.psum_tensor("kvacc1", [128, 288], F32))
                kvsum = esA.enter_context(nc.sbuf_tensor("kvsum", [128, NC, 65], F32))
                psA = [psA0, psA1, psA2, psA3]
                psA_free = [0, 0, 0, 0]
                stats_free = [0]
                kbf_free = [0, 0]
                kvacc_free = [0]
                sumsq_tok = [0]
                xa_free = [0] * (NT + 2)
                xn_tok = [0] * (NT + 1)
                exfree = [0, 0, 0]
                exctr = [0]

                def v65_ones(v):
                    for b in range(2):
                        vop(v, lambda b=b: v.memset(v65[:, b, :, 64:65], 1.0))
                    vop(v, lambda: v.memset(kvsum[:, :, :], 0.0))

                V(v65_ones)

                def g_wq(g):
                    for kc in range(NC):
                        g.dma_start(out=wqbf[:, kc, :],
                                    in_=wqkv.ap()[kc * 128:(kc + 1) * 128, :]).then_inc(s_wq, 16)
                        swq.n += 16

                G(g_wq)

                def a_load(i):
                    if i >= NT:
                        return

                    def y_load(y, i=i):
                        if i >= 2:
                            self.W(y, "y", vch, xa_free[i - 2])
                        dq_xa.issue(y, "y", i,
                                    y.dma_start(out=xtlA[:, i % 2, :, :], in_=xT_t[i]))

                    Y(y_load)

                copy_toks = [[0] * NC for _ in range(NT)]
                sumsA_toks = [[0] * NC for _ in range(NT)]
                sqA_toks = [[0] * NC for _ in range(NT)]
                sumsqA_toks = [[0] * NC for _ in range(NT)]

                def a_copy_piece(i, kc):
                    def s_cp(s, i=i, kc=kc):
                        dq_xa.wait(s, "s", i)
                        if i >= 1 and sumsqA_toks[i - 1][kc]:
                            self.W(s, "s", tch, sumsqA_toks[i - 1][kc])
                        sop(s, lambda: s.activation(
                            xr[:, kc, :], xtlA[:, i % 2, kc, :], ACT.Copy))
                        copy_toks[i][kc] = sch.n

                    S(s_cp)

                def a_sumsT(i):
                    def t_sums(t, i=i):
                        self.W(t, "t", vch, max(setup_tok, stats_free[0]))
                        for kc in range(NC):
                            self.W(t, "t", sch, copy_toks[i][kc])
                            mm = t.matmul(sums[:, :], onesr[:, :], xr[:, kc, :],
                                          start=(kc == 0), stop=(kc == NC - 1))
                            tch.inc(mm)
                            sumsA_toks[i][kc] = tch.n

                    T(t_sums)

                def a_sqS(i):
                    def s_sq(s, i=i):
                        for kc in range(NC):
                            self.W(s, "s", tch, sumsA_toks[i][kc])
                            sop(s, lambda kc=kc: s.activation(
                                xr[:, kc, :], xr[:, kc, :], ACT.Square))
                            sqA_toks[i][kc] = sch.n

                    S(s_sq)

                def a_sumsqT(i):
                    def t_sumsq(t, i=i):
                        for kc in range(NC):
                            self.W(t, "t", sch, sqA_toks[i][kc])
                            mm = t.matmul(sumsq[:, :], onesr[:, :], xr[:, kc, :],
                                          start=(kc == 0), stop=(kc == NC - 1))
                            tch.inc(mm)
                            sumsqA_toks[i][kc] = tch.n

                    T(t_sumsq)
                    sumsq_tok[0] = tch.n

                def a_smalls(i):
                    stats_tok = tch.n

                    def v_ln1(v):
                        self.W(v, "v", tch, stats_tok)
                        vop(v, lambda: v.tensor_scalar(mbA[:, :], sums[:, :], 1.0 / D, None, AL.mult))
                        vop(v, lambda: v.tensor_tensor(t1A[:, :], mbA[:, :], sums[:, :], AL.mult))
                        vop(v, lambda: v.tensor_tensor(t2A[:, :], sumsq[:, :], t1A[:, :], AL.subtract))

                    V(v_ln1)
                    ln1_tok = vch.n
                    stats_free[0] = vch.n

                    def s_sd(s):
                        self.W(s, "s", vch, ln1_tok)
                        sop(s, lambda: s.activation(t2A[:, :], t2A[:, :], ACT.Ln,
                                                    bias=cvec[:, 0:1], scale=cvec[:, 1:2]))
                        sop(s, lambda: s.activation(riA[:, :], t2A[:, :], ACT.Exp,
                                                    bias=0.0, scale=cvec[:, 2:3]))

                    S(s_sd)
                    return sch.n

                def a_apply(i, sd_tok, kc0, kc1):
                    def v_apply(v, i=i):
                        self.W(v, "v", swv, 80)
                        self.W(v, "v", sch, sd_tok)
                        for kc in range(kc0, kc1):
                            vop(v, lambda kc=kc: v.scalar_tensor_tensor(
                                t1A[:, :], xtlA[:, i % 2, kc, :], 1.0, mbA[:, :],
                                AL.mult, AL.subtract))
                            vop(v, lambda kc=kc: v.tensor_tensor(
                                t2A[:, :], t1A[:, :], riA[:, :], AL.mult))
                            vop(v, lambda kc=kc: v.tensor_scalar(
                                xnbf[:, i % 2, kc, :], t2A[:, :], gsb[:, kc:kc + 1],
                                bsb[:, kc:kc + 1], AL.mult, AL.add))

                    V(v_apply)
                    if kc1 == NC:
                        xn_tok[i] = vch.n
                        xa_free[i] = vch.n

                def a_qunits(i, copy_i=None):
                    for ncx in range(NC):
                        slot = ncx % 4

                        def t_q(t, i=i, ncx=ncx, slot=slot):
                            self.W(t, "t", vch, xn_tok[i])
                            self.W(t, "t", swq, 16 * NC)
                            self.W(t, "t", vch, psA_free[slot])
                            for kc in range(NC):
                                mm = t.matmul(psA[slot][:, :],
                                              wqbf[:, kc, ncx * 128:(ncx + 1) * 128],
                                              xnbf[:, i % 2, kc, :],
                                              start=(kc == 0), stop=(kc == NC - 1))
                            tch.inc(mm)

                        T(t_q)
                        q_tok = tch.n
                        eb = exctr[0] % 3
                        exctr[0] += 1

                        def s_exp(s, slot=slot, eb=eb):
                            self.W(s, "s", tch, q_tok)
                            if exfree[eb]:
                                self.W(s, "s", vch, exfree[eb])
                            sop(s, lambda: s.activation(exA[:, eb, :], psA[slot][:, :],
                                                        ACT.Exp))

                        S(s_exp)
                        ex_tok = sch.n

                        def v_q(v, i=i, ncx=ncx, slot=slot, eb=eb):
                            self.W(v, "v", sch, ex_tok)
                            vop(v, lambda: v.tensor_scalar(
                                exA[:, eb, :], exA[:, eb, :], 1.0, None, AL.min))
                            vop(v, lambda: v.scalar_tensor_tensor(
                                qst[:, ncx, i * TT:(i + 1) * TT], psA[slot][:, :], 1.0,
                                exA[:, eb, :], AL.add, AL.max))

                        V(v_q)
                        psA_free[slot] = vch.n
                        exfree[eb] = vch.n
                        if copy_i is not None:
                            a_copy_piece(copy_i, ncx)

                def a_kvgroup(i, g_):
                    ngr = TT // 128
                    gbuf = (i * ngr + g_) % 2
                    for piece in range(4):
                        slot = piece % 4

                        def t_kv(t, i=i, g_=g_, piece=piece, slot=slot, gbuf=gbuf):
                            self.W(t, "t", vch, xn_tok[i])
                            self.W(t, "t", vch, psA_free[slot])
                            if piece == 0:
                                self.W(t, "t", tch, kbf_free[gbuf])
                            for kc in range(NC):
                                mm = t.matmul(psA[slot][:, :],
                                              xnbf[:, i % 2, kc, g_ * 128:(g_ + 1) * 128],
                                              wqkv_rhs(kc, piece),
                                              start=(kc == 0), stop=(kc == NC - 1))
                            tch.inc(mm)

                        def wqkv_rhs(kc, piece):
                            return wqbf[:, kc, D + piece * 512:D + (piece + 1) * 512]

                        T(t_kv)
                        kv_tok = tch.n

                        if piece < 2:
                            eb = exctr[0] % 3
                            exctr[0] += 1

                            def s_exp2(s, slot=slot, eb=eb):
                                self.W(s, "s", tch, kv_tok)
                                if exfree[eb]:
                                    self.W(s, "s", vch, exfree[eb])
                                sop(s, lambda: s.activation(exA[:, eb, :],
                                                            psA[slot][:, :], ACT.Exp))

                            S(s_exp2)
                            ex2 = sch.n

                            def v_k(v, piece=piece, slot=slot, gbuf=gbuf, eb=eb):
                                self.W(v, "v", sch, ex2)
                                vop(v, lambda: v.tensor_scalar(
                                    exA[:, eb, :], exA[:, eb, :], 1.0, None, AL.min))
                                vop(v, lambda: v.scalar_tensor_tensor(
                                    kbf[:, gbuf, piece * 512:(piece + 1) * 512],
                                    psA[slot][:, :], 1.0, exA[:, eb, :], AL.add, AL.max))

                            V(v_k)
                            psA_free[slot] = vch.n
                            exfree[eb] = vch.n
                        else:
                            def v_v(v, piece=piece, slot=slot, gbuf=gbuf):
                                self.W(v, "v", tch, kv_tok)
                                p0 = (piece - 2) * 8
                                vop(v, lambda: v.tensor_copy(
                                    v65[:, gbuf, p0:p0 + 8, 0:64],
                                    psA[slot][:, :].rearrange("p (h j) -> p h j", j=64)))

                            V(v_v)
                            psA_free[slot] = vch.n

                    evac_tok = vch.n

                    def kvmm_emit(gbuf=gbuf, evac_tok=evac_tok):
                        def t_kvmm(t, gbuf=gbuf):
                            self.W(t, "t", vch, evac_tok)
                            self.W(t, "t", vch, kvacc_free[0])
                            for c in range(NC):
                                tgt = kvacc0 if c < 4 else kvacc1
                                cc = c % 4
                                for half in range(2):
                                    h = 2 * c + half
                                    mm = t.matmul(tgt[half * 64:(half + 1) * 64,
                                                      cc * 72:cc * 72 + 65],
                                                  kbf[:, gbuf, h * 64:(h + 1) * 64],
                                                  v65[:, gbuf, h, :],
                                                  start=True, stop=True)
                            tch.inc(mm)

                        T(t_kvmm)
                        kbf_free[gbuf] = tch.n
                        kvmm_tok = tch.n

                        def v_kvacc(v, kvmm_tok=kvmm_tok):
                            self.W(v, "v", tch, kvmm_tok)
                            for half2 in range(2):
                                tgt = kvacc0 if half2 == 0 else kvacc1
                                vop(v, lambda tgt=tgt, half2=half2: v.tensor_add(
                                    kvsum[:, half2 * 4:(half2 + 1) * 4, :],
                                    kvsum[:, half2 * 4:(half2 + 1) * 4, :],
                                    tgt[:, :].rearrange("p (c e) -> p c e", e=72)[:, :, 0:65]))

                        V(v_kvacc)
                        kvacc_free[0] = vch.n
                    return kvmm_emit

                a_load(0)
                a_load(1)
                for kc0 in range(NC):
                    a_copy_piece(0, kc0)
                a_sumsT(0)
                a_sqS(0)
                a_sumsqT(0)
                sd0 = a_smalls(0)
                a_apply(0, sd0, 0, NC)
                pend = [None]
                for i in range(NT):
                    a_qunits(i, copy_i=(i + 1 if i + 1 < NT else None))
                    if i + 1 < NT:
                        a_sumsT(i + 1)
                    for g_ in range(4):
                        ke = a_kvgroup(i, g_)
                        if pend[0] is not None:
                            pend[0]()
                        pend[0] = ke
                        if g_ == 0 and i + 1 < NT:
                            a_sqS(i + 1)
                            a_sumsqT(i + 1)
                            sdn = a_smalls(i + 1)
                        if g_ == 1 and i + 1 < NT:
                            a_apply(i + 1, sdn, 0, NC // 2)
                        if g_ == 2 and i + 1 < NT:
                            a_apply(i + 1, sdn, NC // 2, NC)
                            a_load(i + 2)
                if pend[0] is not None:
                    pend[0]()

                def v_kvev(v):
                    vop(v, lambda: v.tensor_copy(kvks[:, :, :], kvsum[:, :, :]))
                    vop(v, lambda: v.memset(bdg[:, :, :], 0.0))
                    for h in range(16):
                        half = h % 2
                        c = h // 2
                        vop(v, lambda h=h, half=half, c=c: v.tensor_copy(
                            bdg[half * 64:(half + 1) * 64, c, h:h + 1],
                            kvks[half * 64:(half + 1) * 64, c, 64:65]))

                V(v_kvev)

            self._barrier(pb, 5)

            esB = contextlib.ExitStack()
            with esB:
                wor = esB.enter_context(nc.sbuf_tensor("wor", [128, NC, D], F32R))
                xtlB = esB.enter_context(nc.sbuf_tensor("xtlB", [128, 2, NC, TT], F32))
                atr = esB.enter_context(nc.sbuf_tensor("atr", [128, NC, TT], F32R))
                ysb = esB.enter_context(nc.sbuf_tensor("ysb", [128, NC, TT], F32))
                x2tl = esB.enter_context(nc.sbuf_tensor("x2tl", [128, 1, NC, TT], F32))
                yr = atr
                rcd = esB.enter_context(nc.sbuf_tensor("rcd", [16, TT], F32))
                rcdr = esB.enter_context(nc.sbuf_tensor("rcdr", [16, 2, TT], F32R))
                mbB = esB.enter_context(nc.sbuf_tensor("mbB", [128, TT], F32))
                t1B = esB.enter_context(nc.sbuf_tensor("t1B", [128, 2, TT], F32))
                t2B = esB.enter_context(nc.sbuf_tensor("t2B", [128, TT], F32))
                riB = esB.enter_context(nc.sbuf_tensor("riB", [128, TT], F32))
                psB0 = esB.enter_context(nc.psum_tensor("psB0", [128, TT], F32))
                psB1 = esB.enter_context(nc.psum_tensor("psB1", [128, TT], F32))
                psB2 = esB.enter_context(nc.psum_tensor("psB2", [128, TT], F32))
                psB3 = esB.enter_context(nc.psum_tensor("psB3", [128, TT], F32))
                psB4 = esB.enter_context(nc.psum_tensor("psB4", [128, TT], F32))
                denp = esB.enter_context(nc.psum_tensor("denp", [16, TT], F32))
                sumsB = esB.enter_context(nc.psum_tensor("sumsB", [128, TT], F32))
                sumsqB = esB.enter_context(nc.psum_tensor("sumsqB", [128, TT], F32))
                psB = [psB0, psB1, psB2, psB3, psB4]
                psB_free = [(vch, 0)] * 5
                statsB_free = [(vch, 0)]
                denp_free = [(sch, 0)]
                t1f = [(vch, 0), (vch, 0)]
                xb_free = [0] * (NT + 2)
                rcd_tok = [0] * (NT + 2)
                stats_tok = [0] * (NT + 1)
                y_toks = [[0] * NC for _ in range(NT)]

                def g_wo(g):
                    for ec in range(NC):
                        g.dma_start(out=wor[:, ec, :],
                                    in_=wout.ap()[ec * 128:(ec + 1) * 128, :]).then_inc(s_wo, 16)
                        swo.n += 16

                G(g_wo)

                def b_load(i):
                    if i >= NT:
                        return

                    def y_loadB(y, i=i):
                        if i >= 2:
                            self.W(y, "y", vch, xb_free[i - 2])
                        dq_xb.issue(y, "y", i,
                                    y.dma_start(out=xtlB[:, i % 2, :, :], in_=xT_t[i]))

                    Y(y_loadB)

                def b_den(i):
                    if i >= NT:
                        return

                    def t_den(t, i=i):
                        ch0, v0 = denp_free[0]
                        if v0:
                            self.W(t, "t", ch0, v0)
                        for kc in range(NC):
                            mm = t.matmul(denp[:, :], bdg[:, kc, :],
                                          qst[:, kc, i * TT:(i + 1) * TT],
                                          start=(kc == 0), stop=(kc == NC - 1))
                        tch.inc(mm)

                    T(t_den)
                    den_tok = tch.n

                    def s_recip(s, i=i, den_tok=den_tok):
                        self.W(s, "s", tch, den_tok)
                        sop(s, lambda: s.activation(rcd[:, :], denp[:, :], ACT.Ln,
                                                    bias=cvec[0:16, 4:5], scale=1.0))
                        sop(s, lambda: s.activation(rcdr[:, i % 2, :], rcd[:, :],
                                                    ACT.Exp, bias=0.0,
                                                    scale=cvec[0:16, 3:4]))

                    S(s_recip)
                    rcd_tok[i] = sch.n
                    denp_free[0] = (sch, sch.n)

                def b_pairs(i):
                    for c in range(NC):
                        s0 = c % 2
                        s1 = 2 + (c % 2)

                        def t_pair(t, i=i, c=c, s0=s0, s1=s1):
                            ch0, v0 = psB_free[s0]
                            if v0:
                                self.W(t, "t", ch0, v0)
                            for half in range(2):
                                mm = t.matmul(psB[s0][half * 64:(half + 1) * 64, :],
                                              kvks[half * 64:(half + 1) * 64, c, 0:64],
                                              qst[half * 64:(half + 1) * 64, c,
                                                  i * TT:(i + 1) * TT],
                                              start=True, stop=True)
                                tch.inc(mm)
                            self.W(t, "t", swv, 80)
                            self.W(t, "t", sch, rcd_tok[i])
                            ch1, v1 = psB_free[s1]
                            if v1:
                                self.W(t, "t", ch1, v1)
                            mm = t.matmul(psB[s1][:, :],
                                          selE[:, c * 128:(c + 1) * 128],
                                          rcdr[:, i % 2, :], start=True, stop=True)
                            tch.inc(mm)

                        T(t_pair)
                        pair_tok = tch.n

                        def s_dc(s, c=c, s1=s1, pair_tok=pair_tok):
                            self.W(s, "s", tch, pair_tok)
                            ch2, v2 = t1f[c % 2]
                            if v2:
                                self.W(s, "s", ch2, v2)
                            sop(s, lambda: s.activation(t1B[:, c % 2, :],
                                                        psB[s1][:, :], ACT.Copy))

                        S(s_dc)
                        dc_tok = sch.n
                        psB_free[s1] = (sch, dc_tok)

                        def v_div(v, c=c, s0=s0, dc_tok=dc_tok):
                            self.W(v, "v", sch, dc_tok)
                            vop(v, lambda: v.tensor_tensor(
                                atr[:, c, :], psB[s0][:, :], t1B[:, c % 2, :],
                                AL.mult))

                        V(v_div)
                        psB_free[s0] = (vch, vch.n)
                        t1f[c % 2] = (vch, vch.n)

                def b_wout(i, at_tok):
                    wout_last = [0]
                    for fc in range(NC):
                        slot = fc % 5

                        def t_wout(t, fc=fc, slot=slot):
                            self.W(t, "t", vch, at_tok)
                            self.W(t, "t", swo, 16 * NC)
                            ch0, v0 = psB_free[slot]
                            if v0:
                                self.W(t, "t", ch0, v0)
                            for ec in range(NC):
                                mm = t.matmul(psB[slot][:, :],
                                              wor[:, ec, fc * 128:(fc + 1) * 128],
                                              atr[:, ec, :],
                                              start=(ec == 0), stop=(ec == NC - 1))
                            tch.inc(mm)

                        T(t_wout)
                        wout_tok = tch.n
                        wout_last[0] = wout_tok

                        def v_y(v, i=i, fc=fc, slot=slot):
                            self.W(v, "v", tch, wout_tok)
                            dq_xb.wait(v, "v", i)
                            vop(v, lambda: v.scalar_tensor_tensor(
                                ysb[:, fc, :], xtlB[:, i % 2, fc, :], 2.0,
                                psB[slot][:, :], AL.mult, AL.add))

                        V(v_y)
                        psB_free[slot] = (vch, vch.n)
                        y_toks[i][fc] = vch.n
                    xb_free[i] = vch.n
                    return wout_last[0]

                def b_stats(i, wout_all):
                    yr_toks = []
                    sums_toks = []
                    ysq_toks = []

                    def s_yr(s, i=i):
                        self.W(s, "s", tch, wout_all)
                        for kc in range(NC):
                            self.W(s, "s", vch, y_toks[i][kc])
                            sop(s, lambda kc=kc: s.activation(
                                yr[:, kc, :], ysb[:, kc, :], ACT.Copy))
                            yr_toks.append(sch.n)

                    S(s_yr)

                    def t_sumsB(t):
                        ch0, v0 = statsB_free[0]
                        if v0:
                            self.W(t, "t", ch0, v0)
                        for kc in range(NC):
                            self.W(t, "t", sch, yr_toks[kc])
                            mm = t.matmul(sumsB[:, :], onesr[:, :], yr[:, kc, :],
                                          start=(kc == 0), stop=(kc == NC - 1))
                            tch.inc(mm)
                            sums_toks.append(tch.n)

                    T(t_sumsB)

                    def s_ysq(s):
                        for kc in range(NC):
                            self.W(s, "s", tch, sums_toks[kc])
                            sop(s, lambda kc=kc: s.activation(
                                yr[:, kc, :], yr[:, kc, :], ACT.Square))
                            ysq_toks.append(sch.n)

                    S(s_ysq)

                    def t_sumsqB(t):
                        for kc in range(NC):
                            self.W(t, "t", sch, ysq_toks[kc])
                            mm = t.matmul(sumsqB[:, :], onesr[:, :], yr[:, kc, :],
                                          start=(kc == 0), stop=(kc == NC - 1))
                            tch.inc(mm)

                    T(t_sumsqB)
                    stats_tok[i] = tch.n

                def b_tail(i):
                    if i < 0:
                        return

                    def v_ln2(v, i=i):
                        self.W(v, "v", tch, stats_tok[i])
                        vop(v, lambda: v.tensor_scalar(mbB[:, :], sumsB[:, :], 1.0 / D, None, AL.mult))
                        vop(v, lambda: v.tensor_tensor(riB[:, :], mbB[:, :], sumsB[:, :], AL.mult))
                        vop(v, lambda: v.tensor_tensor(t2B[:, :], sumsqB[:, :], riB[:, :], AL.subtract))

                    V(v_ln2)
                    ln2_tok = vch.n
                    statsB_free[0] = (vch, vch.n)

                    def s_sdB(s):
                        self.W(s, "s", vch, ln2_tok)
                        sop(s, lambda: s.activation(t2B[:, :], t2B[:, :], ACT.Ln,
                                                    bias=cvec[:, 0:1], scale=cvec[:, 1:2]))
                        sop(s, lambda: s.activation(riB[:, :], t2B[:, :], ACT.Exp,
                                                    bias=0.0, scale=cvec[:, 2:3]))

                    S(s_sdB)
                    sdB_tok = sch.n

                    def v_x2(v, i=i):
                        self.W(v, "v", sch, sdB_tok)
                        dq_x2w.wait(v, "v", i - 1)
                        for kc in range(NC):
                            vop(v, lambda kc=kc: v.scalar_tensor_tensor(
                                t2B[:, :], ysb[:, kc, :], 1.0, mbB[:, :],
                                AL.mult, AL.subtract))
                            vop(v, lambda kc=kc: v.tensor_tensor(
                                x2tl[:, 0, kc, :], t2B[:, :], riB[:, :], AL.mult))

                    V(v_x2)
                    x2_tok = vch.n

                    def y_x2w(y, i=i):
                        self.W(y, "y", vch, x2_tok)
                        dq_x2w.issue(y, "y", i, y.dma_start(out=x2d_t[i], in_=x2tl[:, 0, :, :]))

                    Y(y_x2w)

                    def v_x2r(v, i=i):
                        vop(v, lambda: v.tensor_reduce(
                            mxac[:, i:i + 1],
                            x2tl[:, 0, :, :].rearrange("p c t -> p (c t)"),
                            AX, AL.max, apply_absolute_value=True))

                    V(v_x2r)

                b_load(0)
                b_load(1)
                b_den(0)
                b_den(1)
                for i in range(NT):
                    b_pairs(i)
                    div_tok = vch.n
                    b_tail(i - 1)
                    wout_all = b_wout(i, div_tok)
                    b_stats(i, wout_all)
                    b_den(i + 2)
                    b_load(i + 2)
                b_tail(NT - 1)

                def v_mx(v):
                    vop(v, lambda: v.tensor_reduce(mxac[:, NT * NC:NT * NC + 1],
                                                   mxac[:, 0:NT], AX, AL.max))

                V(v_mx)
                mx_tok = vch.n

                def g_flip(g):
                    self.W(g, "g", vch, mx_tok)
                    g.dma_start(out=redd.ap().rearrange("(p o) -> p o", o=1),
                                in_=mxac[:, NT * NC:NT * NC + 1]).then_inc(s_gm, 16)
                    sgm.n += 16
                    g.wait_ge(s_gm, sgm.n)
                    g.dma_start(out=mxrow[:, :],
                                in_=redd.ap().rearrange("(o p) -> o p", o=1)).then_inc(s_gm, 16)
                    sgm.n += 16

                G(g_flip)
                flip_tok = sgm.n

                def v_gmax(v):
                    self.W(v, "v", sgm, flip_tok)
                    vop(v, lambda: v.tensor_reduce(mx1[0:1, 0:1], mxrow[:, :], AX, AL.max))

                V(v_gmax)
                gmax_tok = vch.n

                def g_cc(g):
                    self.W(g, "g", vch, gmax_tok)
                    g.dma_start(out=cc_in[:, :], in_=mx1[:, :]).then_inc(s_gm, 16)
                    sgm.n += 16
                    g.wait_ge(s_gm, sgm.n)
                    g.collective_compute(
                        "AllReduce", AL.max,
                        replica_groups=[list(range(self.n_cores))],
                        ins=[cc_in.ap().opt()], outs=[cc_out.ap().opt()],
                    ).then_inc(c_cc, 1)
                    g.wait_ge(c_cc, 1)
                    g.dma_start(out=mx1[:, :], in_=cc_out[:, :]).then_inc(s_gm, 16)
                    sgm.n += 16

                G(g_cc)
                cc_tok = sgm.n

                def v_xsc(v):
                    self.W(v, "v", sgm, cc_tok)
                    vop(v, lambda: v.tensor_scalar(mx1[0:1, 1:2], mx1[0:1, 0:1],
                                                   1.0 / 127.0, Q_EPS, AL.mult, AL.max))
                    vop(v, lambda: v.reciprocal(mx1[0:1, 2:3], mx1[0:1, 1:2]))

                V(v_xsc)
                xsc_tok = vch.n

                def t_xscb(t):
                    self.W(t, "t", vch, xsc_tok)
                    mm = t.matmul(sumsB[:, 0:2], ones[0:1, :],
                                  mx1[0:1, 1:3], start=True, stop=True)
                    tch.inc(mm)

                T(t_xscb)
                xscb_tok = tch.n

                def v_xsc2(v):
                    self.W(v, "v", tch, xscb_tok)
                    vop(v, lambda: v.tensor_copy(xscq[:, :], sumsB[:, 0:2]))
                    self.W(v, "v", swv, 80)
                    vop(v, lambda: v.tensor_scalar(swsc[:, :], wssb[:, :],
                                                   xscq[:, 0:1], None, AL.mult))

                V(v_xsc2)

            esQST.close()

            self._barrier(pb, 10, dqs=(dq_x2w,))

            esC = contextlib.ExitStack()
            with esC:
                wibf = esC.enter_context(nc.sbuf_tensor("wibf", [128, NC, D], BF16))
                x2tC = esC.enter_context(nc.sbuf_tensor("x2tC", [128, 2, NC, TT], F32))
                qsc = esC.enter_context(nc.sbuf_tensor("qsc", [128, NC, TT], F32))
                wibf_ = wibf

                def g_wi(g):
                    for kc in range(NC):
                        g.dma_start(out=wibf_[:, kc, :],
                                    in_=wi8T.ap()[kc * 128:(kc + 1) * 128, :]).then_inc(s_wi, 16)
                        swi.n += 16

                G(g_wi)
                xqbf = esC.enter_context(nc.sbuf_tensor("xqbf", [128, 2, NC, TT], BF16))
                y3 = esC.enter_context(nc.sbuf_tensor("y3", [128, 2, NC, TT], F32))
                y3r = esC.enter_context(nc.sbuf_tensor("y3r", [128, NC, TT], F32R))
                y3sq = esC.enter_context(nc.sbuf_tensor("y3sq", [128, NC, TT], F32R))
                otl = esC.enter_context(nc.sbuf_tensor("otl", [128, 2, NC, TT], F32))
                mbC = esC.enter_context(nc.sbuf_tensor("mbC", [128, TT], F32))
                t1C = esC.enter_context(nc.sbuf_tensor("t1C", [128, TT], F32))
                t2C = esC.enter_context(nc.sbuf_tensor("t2C", [128, TT], F32))
                riC = esC.enter_context(nc.sbuf_tensor("riC", [128, TT], F32))
                psC = [esC.enter_context(nc.psum_tensor("psC%d" % k, [128, TT], F32))
                       for k in range(6)]
                sumsC = esC.enter_context(nc.psum_tensor("sumsC", [128, TT], F32))
                sumsqC = esC.enter_context(nc.psum_tensor("sumsqC", [128, TT], F32))
                psC_free = [0] * 6
                statsC_free = [(vch, 0)]
                x2c_free = [0] * (NT + 2)
                q_tok = [0] * (NT + 1)
                statsC_tok = [0] * (NT + 1)
                epi_toks = [[0] * NC for _ in range(NT)]
                y3r_toks = [[0] * NC for _ in range(NT)]
                sumsT_toks = [[0] * NC for _ in range(NT)]
                sumsqT_toks = [[0] * NC for _ in range(NT)]
                sdC_toks = [0] * NT
                sq_toks = [[0] * NC for _ in range(NT)]
                vq_toks = [[0] * NC for _ in range(NT)]
                qs_toks = [[0] * NC for _ in range(NT)]

                def c_load(i):
                    if i >= NT:
                        return

                    def y_loadC(y, i=i):
                        dq_x2w.wait(y, "y", i)
                        if i >= 2:
                            self.W(y, "y", vch, x2c_free[i - 2])
                        dq_x2r.issue(y, "y", i,
                                     y.dma_start(out=x2tC[:, i % 2, :, :], in_=x2d_t[i]))

                    Y(y_loadC)

                def c_quant_v(i):
                    if i >= NT:
                        return

                    def v_q(v, i=i):
                        dq_x2r.wait(v, "v", i)
                        for kc in range(NC):
                            if i >= 1 and qs_toks[i - 1][kc]:
                                self.W(v, "v", sch, qs_toks[i - 1][kc])
                            vop(v, lambda kc=kc: v.tensor_scalar(
                                qsc[:, kc, :], x2tC[:, i % 2, kc, :],
                                xscq[:, 1:2], MAGIC, AL.mult, AL.add))
                            vq_toks[i][kc] = vch.n

                    V(v_q)

                def c_quant_s(i):
                    if i >= NT:
                        return

                    def s_q(s, i=i):
                        for kc in range(NC):
                            self.W(s, "s", vch, vq_toks[i][kc])
                            sop(s, lambda kc=kc: s.activation(
                                xqbf[:, i % 2, kc, :], qsc[:, kc, :],
                                ACT.Identity, bias=cvec[:, 5:6], scale=1.0))
                            qs_toks[i][kc] = sch.n

                    S(s_q)
                    q_tok[i] = sch.n

                def c_ffn(i):
                    for ncx in range(NC):
                        slot = ncx % 6

                        def t_ffn(t, i=i, ncx=ncx, slot=slot):
                            self.W(t, "t", sch, q_tok[i])
                            self.W(t, "t", swi, 16 * NC)
                            self.W(t, "t", vch, psC_free[slot])
                            for kc in range(NC):
                                mm = t.matmul(psC[slot][:, :],
                                              wibf[:, kc, ncx * 128:(ncx + 1) * 128],
                                              xqbf[:, i % 2, kc, :],
                                              start=(kc == 0), stop=(kc == NC - 1))
                            tch.inc(mm)

                        T(t_ffn)
                        ffn_tok = tch.n

                        def v_epi(v, i=i, ncx=ncx, slot=slot):
                            self.W(v, "v", tch, ffn_tok)
                            if i >= 2 and y3r_toks[i - 2][ncx]:
                                self.W(v, "v", sch, y3r_toks[i - 2][ncx])
                            vop(v, lambda: v.scalar_tensor_tensor(
                                y3[:, i % 2, ncx, :], psC[slot][:, :],
                                swsc[:, ncx:ncx + 1],
                                x2tC[:, i % 2, ncx, :], AL.mult, AL.add))

                        V(v_epi)
                        psC_free[slot] = vch.n
                        epi_toks[i][ncx] = vch.n
                    x2c_free[i] = vch.n

                def c_stats_s(i):
                    def s_y3r(s, i=i):
                        for kc in range(NC):
                            self.W(s, "s", vch, epi_toks[i][kc])
                            if i >= 1 and sumsT_toks[i - 1][kc]:
                                self.W(s, "s", tch, sumsT_toks[i - 1][kc])
                            sop(s, lambda kc=kc: s.activation(
                                y3r[:, kc, :], y3[:, i % 2, kc, :],
                                ACT.Identity, bias=fbsb[:, kc:kc + 1], scale=1.0))
                            y3r_toks[i][kc] = sch.n
                            if i >= 1 and sumsqT_toks[i - 1][kc]:
                                self.W(s, "s", tch, sumsqT_toks[i - 1][kc])
                            sop(s, lambda kc=kc: s.activation(
                                y3sq[:, kc, :], y3r[:, kc, :], ACT.Square))
                            sq_toks[i][kc] = sch.n

                    S(s_y3r)

                def c_stats_t(i):
                    if i < 0:
                        return

                    def t_sumsC(t, i=i):
                        ch0, v0 = statsC_free[0]
                        if v0:
                            self.W(t, "t", ch0, v0)
                        for kc in range(NC):
                            self.W(t, "t", sch, y3r_toks[i][kc])
                            mm = t.matmul(sumsC[:, :], onesr[:, :], y3r[:, kc, :],
                                          start=(kc == 0), stop=(kc == NC - 1))
                            tch.inc(mm)
                            sumsT_toks[i][kc] = tch.n
                        for kc in range(NC):
                            self.W(t, "t", sch, sq_toks[i][kc])
                            mm = t.matmul(sumsqC[:, :], onesr[:, :], y3sq[:, kc, :],
                                          start=(kc == 0), stop=(kc == NC - 1))
                            tch.inc(mm)
                            sumsqT_toks[i][kc] = tch.n

                    T(t_sumsC)
                    statsC_tok[i] = tch.n

                def c_tail_ln(i):
                    if i < 0:
                        return

                    def v_ln3(v, i=i):
                        self.W(v, "v", tch, statsC_tok[i])
                        vop(v, lambda: v.tensor_scalar(mbC[:, :], sumsC[:, :], 1.0 / D, None, AL.mult))
                        vop(v, lambda: v.tensor_tensor(t1C[:, :], mbC[:, :], sumsC[:, :], AL.mult))
                        vop(v, lambda: v.tensor_tensor(t2C[:, :], sumsqC[:, :], t1C[:, :], AL.subtract))

                    V(v_ln3)
                    ln3_tok = vch.n
                    statsC_free[0] = (vch, vch.n)

                    def s_sdC(s):
                        self.W(s, "s", vch, ln3_tok)
                        sop(s, lambda: s.activation(t2C[:, :], t2C[:, :], ACT.Ln,
                                                    bias=cvec[:, 0:1], scale=cvec[:, 1:2]))
                        sop(s, lambda: s.activation(riC[:, :], t2C[:, :], ACT.Exp,
                                                    bias=0.0, scale=cvec[:, 2:3]))

                    S(s_sdC)
                    sdC_toks[i] = sch.n

                def c_tail_out(i):
                    if i < 0:
                        return
                    sdC_tok = sdC_toks[i]

                    def v_out(v, i=i):
                        self.W(v, "v", sch, sdC_tok)
                        dq_out.wait(v, "v", i - 2)
                        for kc in range(NC):
                            vop(v, lambda kc=kc, i=i: v.scalar_tensor_tensor(
                                t1C[:, :], y3[:, i % 2, kc, :], fbsb[:, kc:kc + 1],
                                mbC[:, :], AL.add, AL.subtract))
                            vop(v, lambda kc=kc, i=i: v.tensor_tensor(
                                otl[:, i % 2, kc, :], t1C[:, :], riC[:, :], AL.mult))

                    V(v_out)
                    out_tok = vch.n

                    def y_store(y, i=i):
                        self.W(y, "y", vch, out_tok)
                        dq_out.issue(y, "y", i, y.dma_start(out=outT_t[i], in_=otl[:, i % 2, :, :]))

                    Y(y_store)

                c_load(0)
                c_load(1)
                c_quant_v(0)
                c_quant_s(0)
                for i in range(NT):
                    c_ffn(i)
                    c_quant_v(i + 1)
                    c_quant_s(i + 1)
                    c_stats_t(i - 1)
                    c_tail_ln(i - 1)
                    c_stats_s(i)
                    c_tail_out(i - 1)
                    c_load(i + 2)
                c_stats_t(NT - 1)
                c_tail_ln(NT - 1)
                c_tail_out(NT - 1)

            def g_fin(g):
                for dq in (dq_out, dq_xa, dq_xb, dq_x2w, dq_x2r):
                    for sem, n in dq.total():
                        if n:
                            g.wait_ge(sem, n)
                for ch in (swv, swq, swo, swi, sgm):
                    if ch.n:
                        g.wait_ge(ch.sem, ch.n)

            G(g_fin)

        return nc

    def _barrier(self, pb, target, dqs=()):
        chains = {"t": self.tch, "v": self.vch, "s": self.sch}
        for name, sec in (("t", self.block.tensor), ("v", self.block.vector),
                          ("s", self.block.scalar), ("g", self.block.gpsimd),
                          ("y", self.block.sync)):
            def f(eng, pb=pb, name=name):
                ch = chains.get(name)
                if ch is not None:
                    self.W(eng, name, ch)
                if name == "y":
                    for dq in dqs:
                        for sem, n in dq.total():
                            if n:
                                eng.wait_ge(sem, n)
                eng.nop().then_inc(pb.sem, 1)
            sec(f)
        pb.n += 5

        for name, sec in (("t", self.block.tensor), ("v", self.block.vector),
                          ("s", self.block.scalar), ("g", self.block.gpsimd),
                          ("y", self.block.sync)):
            def f(eng, pb=pb, target=target):
                eng.wait_ge(pb.sem, target)
            sec(f)


def build_nc(L=4096, n_cores=8):
    return Builder(L, n_cores).build()


_NC_CACHE = {}


def kernel(L0_attn_x, L0_attn_gamma, L0_attn_beta, L0_attn_Wqkv, L0_attn_Wout,
           L0_ffn_weight_int8, L0_ffn_weight_scale, L0_ffn_bias):
    x = np.asarray(L0_attn_x)
    B, L, Dm = x.shape
    assert Dm == D
    n_cores = 8
    key = (L, n_cores)
    if key not in _NC_CACHE:
        _NC_CACHE[key] = build_nc(L=L, n_cores=n_cores)
    nc = _NC_CACHE[key]

    wqkv = np.ascontiguousarray(np.asarray(L0_attn_Wqkv, dtype=np.float32))
    wout = np.ascontiguousarray(np.asarray(L0_attn_Wout, dtype=np.float32))
    wi8T = np.ascontiguousarray(np.asarray(L0_ffn_weight_int8).T)
    gam = np.ascontiguousarray(np.asarray(L0_attn_gamma, dtype=np.float32))
    bet = np.ascontiguousarray(np.asarray(L0_attn_beta, dtype=np.float32))
    fbias = np.ascontiguousarray(np.asarray(L0_ffn_bias, dtype=np.float32))
    wsc = np.ascontiguousarray(np.asarray(L0_ffn_weight_scale, dtype=np.float32).reshape(-1))

    selc = np.zeros((16, D), dtype=np.float32)
    for c in range(NC):
        selc[2 * c, c * 128:c * 128 + 64] = 1.0
        selc[2 * c + 1, c * 128 + 64:(c + 1) * 128] = 1.0

    in_maps = []
    for c in range(n_cores):
        in_maps.append({
            "xT": np.ascontiguousarray(x[c].T.astype(np.float32, copy=False)),
            "wqkv": wqkv, "wout": wout, "wi8T": wi8T,
            "gam": gam, "bet": bet, "fbias": fbias, "wsc": wsc, "selc": selc,
        })
    res = run_bass_kernel_spmd(nc, in_maps, core_ids=list(range(n_cores)))
    out = np.stack([np.ascontiguousarray(r["outT"].T) for r in res.results], axis=0)
    return out.astype(np.float32, copy=False)
